# version 20
# baseline (speedup 1.0000x reference)
import numpy as np
import ml_dtypes
from contextlib import ExitStack

import concourse.bass as bass
import concourse.mybir as mybir
from concourse.bass_utils import run_bass_kernel_spmd

F32 = mybir.dt.float32
BF16 = mybir.dt.bfloat16
AF = mybir.ActivationFunctionType
AX = mybir.AxisListType
ALU = mybir.AluOpType

B, S, D = 2, 4096, 4096
H, DH, DR = 32, 128, 64
DC, DC1 = 512, 1536
QT = 1024
SCALER = 1.0 / float(np.sqrt(DH + DR))
EPS = 1e-5
NDC = D // 128


class Sem:
    def __init__(self, ctx, nc, name):
        self.h = ctx.enter_context(nc.semaphore(name))
        self.v = 0

    def inc(self, instr, n=1):
        instr.then_inc(self.h, n)
        self.v += n
        return self.v


class Waiter:

    def __init__(self):
        self.marks = {}

    def wait(self, eng, sem, val=None):
        v = sem.v if val is None else val
        if v <= 0:
            return
        key = (str(getattr(eng, "engine", type(eng).__name__)), sem.h.num)
        if self.marks.get(key, -1) < v:
            eng.wait_ge(sem.h, v)
            self.marks[key] = v


def build_program(taps=(), upto="p4"):
    nc = bass.Bass("TRN2", target_bir_lowering=False)
    taps = set(taps)

    xb = nc.dram_tensor("xb", [S, D], F32, kind="ExternalInput").ap()
    xq = nc.dram_tensor("xq", [QT, D], F32, kind="ExternalInput").ap()
    xTq = nc.dram_tensor("xTq", [D, QT], F32, kind="ExternalInput").ap()
    w_dq = nc.dram_tensor("w_dq", [3, 128, NDC * 512], F32, kind="ExternalInput").ap()
    b_dq = nc.dram_tensor("b_dq", [128, 12], F32, kind="ExternalInput").ap()
    w_uq = nc.dram_tensor("w_uq", [8, 128, 12 * 512], F32, kind="ExternalInput").ap()
    b_uq = nc.dram_tensor("b_uq", [128, NDC], F32, kind="ExternalInput").ap()
    w_dkv = nc.dram_tensor("w_dkv", [128, NDC * DC], F32, kind="ExternalInput").ap()
    b_dkv = nc.dram_tensor("b_dkv", [128, 4], F32, kind="ExternalInput").ap()
    w_uk = nc.dram_tensor("w_uk", [128, 4 * D], F32, kind="ExternalInput").ap()
    b_uk = nc.dram_tensor("b_uk", [128, NDC], F32, kind="ExternalInput").ap()
    w_uv = nc.dram_tensor("w_uv", [128, 4 * D], F32, kind="ExternalInput").ap()
    b_uv = nc.dram_tensor("b_uv", [128, NDC], F32, kind="ExternalInput").ap()
    w_rq = nc.dram_tensor("w_rq", [4, 128, 12 * 512], F32, kind="ExternalInput").ap()
    b_rq = nc.dram_tensor("b_rq", [128, 16], F32, kind="ExternalInput").ap()
    w_rk = nc.dram_tensor("w_rk", [128, NDC * DR], F32, kind="ExternalInput").ap()
    b_rk = nc.dram_tensor("b_rk", [64, 1], F32, kind="ExternalInput").ap()
    w_out = nc.dram_tensor("w_out", [NDC, 128, NDC * 128], F32, kind="ExternalInput").ap()
    b_out = nc.dram_tensor("b_out", [128, NDC], F32, kind="ExternalInput").ap()
    WT = nc.dram_tensor("WT", [NDC, 128, NDC * 128], F32, kind="ExternalInput").ap()
    fbias = nc.dram_tensor("fbias", [128, NDC], F32, kind="ExternalInput").ap()
    A_T = nc.dram_tensor("A_T", [128, NDC * 256], F32, kind="ExternalInput").ap()
    Bf_T = nc.dram_tensor("Bf_T", [128, 2 * D], F32, kind="ExternalInput").ap()
    cs_rep = nc.dram_tensor("cs_rep", [128, 2], F32, kind="ExternalInput").ap()
    sc_rep = nc.dram_tensor("sc_rep", [128, 2], F32, kind="ExternalInput").ap()
    ident = nc.dram_tensor("ident", [128, 128], BF16, kind="ExternalInput").ap()

    outT = nc.dram_tensor("outT", [D, QT], F32, kind="ExternalOutput").ap()

    sc_KT = nc.dram_tensor("sc_KT", [D, S], BF16, kind="Internal").ap()
    sc_V = nc.dram_tensor("sc_V", [S, D], BF16, kind="Internal").ap()
    sc_xnT = nc.dram_tensor("sc_xnT", [D, QT], BF16, kind="Internal").ap()

    tap_aps = {}

    def tap(name, shape, dtype=F32):
        if name in taps:
            tap_aps[name] = nc.dram_tensor("tap_" + name, shape, dtype,
                                           kind="ExternalOutput").ap()
            return tap_aps[name]
        return None

    t_ckv = tap("ckv", [128, 4 * S], BF16)
    t_kr = tap("kr", [128, S], BF16)
    t_xnq = tap("xnq", [128, NDC * QT], BF16)
    t_cq = tap("cq", [128, 12 * QT], BF16)
    t_qt = tap("qt", [128, H * QT], BF16)
    t_qrt = tap("qrt", [128, 16 * QT], BF16)
    t_negu = tap("negu", [128, H])
    t_attn = tap("attn", [128, H * QT], BF16)
    t_xa = tap("xa", [128, 2 * QT], BF16)
    t_ab = tap("ab", [128, 2 * NDC])
    t_lnst = tap("lnst", [128, 16])

    W = Waiter()
    with ExitStack() as top:
        d_sy = Sem(top, nc, "d_sy")
        d_sc = Sem(top, nc, "d_sc")
        d_gp = Sem(top, nc, "d_gp")
        e_t = Sem(top, nc, "e_t")
        e_v = Sem(top, nc, "e_v")
        e_s = Sem(top, nc, "e_s")
        e_g = Sem(top, nc, "e_g")
        d_st = Sem(top, nc, "d_st")
        dx = [Sem(top, nc, f"dx{i}") for i in range(3)]
        kv = [Sem(top, nc, f"kv{i}") for i in range(2)]
        gw = [Sem(top, nc, f"gw{i}") for i in range(2)]
        xs = [Sem(top, nc, f"xs{i}") for i in range(2)]
        qr = Sem(top, nc, "qr")

        _bump = [16640]

        def sb(name, shape, dtype, off=None):
            nbytes = shape[1] * np.dtype(mybir.dt.np(dtype)).itemsize
            if off is None:
                off = (_bump[0] + 31) & ~31
                _bump[0] = off + nbytes
                assert _bump[0] <= 18464, (name, _bump[0])
            return nc.alloc_sbuf_tensor_at(name, shape, dtype, offset=off)

        identb = sb("identb", [128, 128], BF16)
        ones128 = sb("ones128", [128, 128], BF16)
        nurow = sb("nurow", [1, H], BF16)
        cvec = sb("cvec", [128, 4], F32)
        b_dq_t = sb("b_dq_t", [128, 12], F32)
        b_uq_t = sb("b_uq_t", [128, NDC], F32)
        b_rq_t = sb("b_rq_t", [128, 16], F32)
        b_dkv_t = sb("b_dkv_t", [128, 4], BF16)
        b_uk_t = sb("b_uk_t", [128, NDC], F32)
        b_rk_t = sb("b_rk_t", [64, 1], F32)
        vbias = sb("vbias", [128, NDC], F32)
        bsum = sb("bsum", [128, NDC], F32)
        wscale = sb("wscale", [128, 2], F32)
        negU = sb("negU", [128, H], F32)
        aQ = sb("aQ", [128, H], F32)
        aQr = sb("aQr", [128, 16], F32)
        aK = sb("aK", [128, H], F32)
        aKr = sb("aKr", [128, 1], F32)
        Kr2 = sb("Kr2", [128, S], BF16, 18464)

        with nc.Block("p0") as blk:
            @blk.sync
            def _(sy):
                for dst, src in ((b_dq_t, b_dq), (b_uq_t, b_uq), (b_rq_t, b_rq),
                                 (b_uk_t, b_uk), (vbias, b_uv), (bsum, b_out)):
                    sy.dma_start(out=dst[:, :], in_=src[:, :]).then_inc(d_sy.h, 16)
                    d_sy.v += 16
                sy.dma_start(out=identb[:, :], in_=ident[:, :]).then_inc(d_sy.h, 16)
                sy.dma_start(out=b_rk_t[:, :], in_=b_rk[:, :]).then_inc(d_sy.h, 16)
                d_sy.v += 32

            @blk.scalar
            def _(sc):
                sc.dma_start(out=wscale[:, 0:2], in_=cs_rep[:, :]).then_inc(d_sc.h, 16)
                sc.dma_start(out=negU[:, 0:2], in_=sc_rep[:, :]).then_inc(d_sc.h, 16)
                sc.dma_start(out=aK[:, 0:NDC], in_=fbias[:, :]).then_inc(d_sc.h, 16)
                d_sc.v += 48

            @blk.gpsimd
            def _(g):
                g.dma_start(out=b_dkv_t[:, :], in_=b_dkv[:, :]).then_inc(d_gp.h, 16)
                d_gp.v += 16
                g.memset(ones128[:, :], 1.0)
                g.memset(cvec[:, 0:1], EPS)
                g.memset(cvec[:, 1:2], 1.0 / D)
                ins = g.memset(cvec[:, 2:3], -1.0)
                e_g.inc(ins)

            @blk.vector
            def _(v):
                W.wait(v, d_sc)
                W.wait(v, d_sy)
                ins = v.tensor_tensor(out=wscale[:, :], in0=wscale[:, :],
                                      in1=negU[:, 0:2], op=ALU.mult)
                e_v.inc(ins)
                ins = v.tensor_tensor(out=bsum[:, :], in0=bsum[:, :],
                                      in1=aK[:, 0:NDC], op=ALU.add)
                e_v.inc(ins)

        p0_sy = d_sy.v

        def ln_transpose(blk, src_ap, n_tiles, xf, xn_, xnT_of, dump,
                         ssum, ssq, rstd, negmr, ptr, ptr2, pair_consumer):
            tile_ln_done = {}
            tr_mm_done = {}
            grp_evac = {}
            nxf = len(xf)
            s_base = {}
            g_base = {}
            for ti in range(n_tiles):
                tt = ti % 4
                buf = ti % nxf
                nbuf = ti % 2
                s_base[ti] = e_s.v
                g_base[ti] = e_g.v
                s_sq = e_s.v + 1
                s_exp = e_s.v + 3
                g_var = e_g.v + 4
                g_negmr = e_g.v + 6

                @blk.sync
                def _(sy, ti=ti, buf=buf):
                    if ti >= nxf:
                        W.wait(sy, e_s, tile_ln_done[ti - nxf])
                    sy.dma_start(out=xf[buf][:, :],
                                 in_=src_ap[ti * 128:(ti + 1) * 128, :]
                                 ).then_inc(dx[buf].h, 16)
                    dx[buf].v += 16

                @blk.vector
                def _(v, ti=ti, tt=tt, buf=buf):
                    W.wait(v, dx[buf])
                    if ti >= 4:
                        W.wait(v, e_g, g_base[ti - 4])
                    ins = v.reduce_sum(ssum[:, tt:tt + 1], xf[buf][:, :], axis=AX.X)
                    e_v.inc(ins)

                @blk.scalar
                def _(sc, ti=ti, tt=tt, buf=buf, nbuf=nbuf,
                      g_var=g_var, g_negmr=g_negmr):
                    c = slice(tt, tt + 1)
                    W.wait(sc, dx[buf])
                    if ti >= 2:
                        W.wait(sc, e_t, tr_mm_done[ti - 2])
                    if ti >= 4:
                        W.wait(sc, e_g, g_base[ti - 4])
                    ins = sc.activation(xn_[nbuf][:, :], xf[buf][:, :], AF.Square,
                                        accum_out=ssq[:, c])
                    e_s.inc(ins)
                    W.wait(sc, e_g, g_var)
                    ins = sc.activation(rstd[:, c], rstd[:, c], AF.Ln, bias=cvec[:, 0:1])
                    m = e_s.inc(ins)
                    W.wait(sc, e_s, m)
                    ins = sc.activation(rstd[:, c], rstd[:, c], AF.Exp, scale=-0.5)
                    e_s.inc(ins)
                    W.wait(sc, e_g, g_negmr)
                    ins = sc.activation(xn_[nbuf][:, :], xf[buf][:, :], AF.Identity,
                                        bias=negmr[:, c], scale=rstd[:, c])
                    tile_ln_done[ti] = e_s.inc(ins)

                @blk.gpsimd
                def _(g, ti=ti, tt=tt, s_sq=s_sq, s_exp=s_exp):
                    c = slice(tt, tt + 1)
                    W.wait(g, e_v)
                    W.wait(g, e_s, s_sq)
                    W.wait(g, e_g)
                    ins = g.tensor_tensor(out=ssum[:, c], in0=ssum[:, c],
                                          in1=cvec[:, 1:2], op=ALU.mult)
                    e_g.inc(ins)
                    W.wait(g, e_g)
                    ins = g.tensor_tensor(out=negmr[:, c], in0=ssq[:, c],
                                          in1=cvec[:, 1:2], op=ALU.mult)
                    e_g.inc(ins)
                    W.wait(g, e_g)
                    ins = g.tensor_tensor(out=rstd[:, c], in0=ssum[:, c],
                                          in1=ssum[:, c], op=ALU.mult)
                    e_g.inc(ins)
                    W.wait(g, e_g)
                    ins = g.tensor_tensor(out=rstd[:, c], in0=negmr[:, c],
                                          in1=rstd[:, c], op=ALU.subtract)
                    e_g.inc(ins)
                    W.wait(g, e_s, s_exp)
                    ins = g.tensor_tensor(out=negmr[:, c], in0=ssum[:, c],
                                          in1=rstd[:, c], op=ALU.mult)
                    e_g.inc(ins)
                    W.wait(g, e_g)
                    ins = g.tensor_tensor(out=negmr[:, c], in0=negmr[:, c],
                                          in1=cvec[:, 2:3], op=ALU.mult)
                    e_g.inc(ins)
                g_base[ti] = e_g.v

                dst, off, stride = xnT_of(ti)
                for g8 in range(8):
                    gid = ti * 8 + g8

                    @blk.tensor
                    def _(t, ti=ti, nbuf=nbuf, g8=g8, gid=gid):
                        W.wait(t, d_sy, p0_sy)
                        W.wait(t, e_s, tile_ln_done[ti])
                        if gid >= 2:
                            W.wait(t, e_v, grp_evac[gid - 2])
                        pt = ptr if g8 % 2 == 0 else ptr2
                        for j in range(4):
                            dc = g8 * 4 + j
                            ins = t.matmul(pt[:, j * 128:(j + 1) * 128],
                                           xn_[nbuf][:, dc * 128:(dc + 1) * 128],
                                           identb[:, :], start=True, stop=True)
                        tr_mm_done[ti] = e_t.inc(ins)

                    @blk.vector
                    def _(v, ti=ti, g8=g8, gid=gid, dst=dst, off=off, stride=stride):
                        W.wait(v, e_t, tr_mm_done[ti])
                        pt = ptr if g8 % 2 == 0 else ptr2
                        for j in range(4):
                            dc = g8 * 4 + j
                            ins = v.tensor_copy(
                                out=dst[:, dc * stride + off: dc * stride + off + 128],
                                in_=pt[:, j * 128:(j + 1) * 128])
                        grp_evac[gid] = e_v.inc(ins)
                if ti % 8 == 7:
                    pair_consumer(ti // 8)

        CKV = sb("CKV", [128, 4 * S], BF16, 26656)

        with ExitStack() as ph:
            E1 = ph.enter_context
            wdkv = sb("wdkv", [128, NDC * DC], BF16, 59648)
            wrk = sb("wrk", [128, NDC * DR], BF16, 92416)
            xf = [sb(f"xfa{i}", [128, D], F32, (96512, 112896, 211200)[i]) for i in range(3)]
            xn_ = [sb(f"xna{i}", [128, D], BF16, 129280 + i * 8192) for i in range(2)]
            xnT = [sb(f"xnTa{i}", [128, NDC * 512], BF16, 145664 + i * 32768)
                   for i in range(2)]
            dump = sb("dumpa", [128, D], F32, 211200)
            ssum = sb("ssuma", [128, 4], F32, 227584)
            ssq = sb("ssqa", [128, 4], F32, 227648)
            rstd = sb("rstda", [128, 4], F32, 227712)
            negmr = sb("negmra", [128, 4], F32, 227776)
            ptr = E1(nc.psum_tensor("ptra", [128, 512], F32))
            ptr2 = E1(nc.psum_tensor("ptr2a", [128, 512], F32))
            pck = [E1(nc.psum_tensor(f"pcka{i}", [128, 1024], F32)) for i in range(2)]
            pkr = E1(nc.psum_tensor("pkra", [64, 1024], F32))

            ck_evac = {"v": 0, "kr": 0}

            def consumers(pair):
                for ccp in range(2):
                    @blk.tensor
                    def _(t, ccp=ccp):
                        W.wait(t, d_gp)
                        W.wait(t, e_v)
                        W.wait(t, e_s, ck_evac["v"])
                        for cc in (2 * ccp, 2 * ccp + 1):
                            for dc in range(NDC):
                                for hf in range(2):
                                    ins = t.matmul(
                                        pck[cc % 2][:, hf * 512:hf * 512 + 512],
                                        wdkv[:, dc * DC + cc * 128: dc * DC + cc * 128 + 128],
                                        xnT[hf][:, dc * 512:(dc + 1) * 512],
                                        start=(dc == 0), stop=(dc == NDC - 1))
                            e_t.inc(ins)

                    @blk.scalar
                    def _(sc, ccp=ccp, pair=pair):
                        base = e_t.v - 2
                        for i, cc in enumerate((2 * ccp, 2 * ccp + 1)):
                            W.wait(sc, e_t, base + i + 1)
                            ins = sc.activation(
                                CKV[:, cc * S + pair * 1024: cc * S + pair * 1024 + 1024],
                                pck[cc % 2][:, :], AF.Identity)
                            ck_evac["v"] = e_s.inc(ins)

                @blk.tensor
                def _(t, pair=pair):
                    W.wait(t, e_s, ck_evac["kr"])
                    for dc in range(NDC):
                        for hf in range(2):
                            ins = t.matmul(pkr[:, hf * 512:hf * 512 + 512],
                                           wrk[:, dc * DR:(dc + 1) * DR],
                                           xnT[hf][:, dc * 512:(dc + 1) * 512],
                                           start=(dc == 0), stop=(dc == NDC - 1))
                    e_t.inc(ins)

                @blk.scalar
                def _(sc, pair=pair):
                    W.wait(sc, d_sy, p0_sy)
                    W.wait(sc, e_t)
                    ins = sc.activation(Kr2[0:64, pair * 1024:(pair + 1) * 1024],
                                        pkr[0:64, :], AF.Identity,
                                        bias=b_rk_t[0:64, 0:1])
                    ck_evac["kr"] = e_s.inc(ins)

            with nc.Block("p1a") as blk:
                @blk.gpsimd
                def _(g):
                    g.dma_start(out=wdkv[:, :], in_=w_dkv[:, :]).then_inc(d_gp.h, 16)
                    g.dma_start(out=wrk[:, :], in_=w_rk[:, :]).then_inc(d_gp.h, 16)
                    d_gp.v += 32

                def xnT_of(ti):
                    return xnT[(ti // 4) % 2], (ti % 4) * 128, 512

                import os
                if not os.environ.get("SKIP_P1A"):
                    ln_transpose(blk, xb, 32, xf, xn_, xnT_of, dump,
                                 ssum, ssq, rstd, negmr, ptr, ptr2, consumers)

                @blk.sync
                def _(sy):
                    W.wait(sy, e_s)
                    sy.dma_start(out=Kr2[64:128, :], in_=Kr2[0:64, :]).then_inc(d_sy.h, 16)
                    d_sy.v += 16
                    if t_ckv is not None:
                        sy.dma_start(out=t_ckv, in_=CKV[:, :]).then_inc(d_sy.h, 16)
                        d_sy.v += 16
                    if t_kr is not None:
                        W.wait(sy, d_sy)
                        sy.dma_start(out=t_kr, in_=Kr2[:, :]).then_inc(d_sy.h, 16)
                        d_sy.v += 16

        if upto == "p1a":
            return nc, tap_aps

        QT_ = sb("QT_", [128, H * QT], BF16, 59648)
        QrT = sb("QrT", [128, 16 * QT], BF16, 196576)

        with ExitStack() as ph:
            E1 = ph.enter_context
            xnq = sb("xnq", [128, NDC * QT], BF16, 59648)
            CQ = sb("CQ", [128, 12 * QT], BF16, 125184)

            with ExitStack() as ph2:
                E2 = ph2.enter_context
                xf = [sb(f"xfb{i}", [128, D], F32, 125184 + i * 16384) for i in range(2)]
                xn_ = [sb(f"xnb{i}", [128, D], BF16, 157952 + i * 8192) for i in range(2)]
                dump = sb("dumpb", [128, D], F32, 174592)
                ssum = sb("ssumb", [128, 4], F32, 174336)
                ssq = sb("ssqb", [128, 4], F32, 174400)
                rstd = sb("rstdb", [128, 4], F32, 174464)
                negmr = sb("negmrb", [128, 4], F32, 174528)
                ptr = E2(nc.psum_tensor("ptrb", [128, 512], F32))
                ptr2 = E2(nc.psum_tensor("ptr2b", [128, 512], F32))

                with nc.Block("p1b_ln") as blk:
                    ln_transpose(blk, xq, 8, xf, xn_,
                                 lambda ti: (xnq, ti * 128, QT), dump,
                                 ssum, ssq, rstd, negmr, ptr, ptr2, lambda p: None)

                    @blk.sync
                    def _(sy):
                        W.wait(sy, e_v)
                        sy.dma_start(out=sc_xnT.rearrange("(dc p) q -> p dc q", p=128),
                                     in_=xnq.rearrange("p (dc q) -> p dc q", dc=NDC)
                                     ).then_inc(d_st.h, 16)
                        d_st.v += 16
                        if t_lnst is not None:
                            W.wait(sy, e_s)
                            for i, src in enumerate((ssum, ssq, rstd, negmr)):
                                sy.dma_start(out=t_lnst[:, i * 4:(i + 1) * 4],
                                             in_=src[:, :]).then_inc(d_sy.h, 16)
                                d_sy.v += 16
                        if t_xnq is not None:
                            sy.dma_start(out=t_xnq, in_=xnq[:, :]).then_inc(d_sy.h, 16)
                            d_sy.v += 16

            if upto == "p1b_ln":
                return nc, tap_aps

            blk_mid = nc.Block("mid")
            blk_mid.__enter__()
            with ExitStack() as ph2:
                E2 = ph2.enter_context
                wdq = [sb(f"wdq{i}", [128, NDC * 512], BF16, 149760 + i * 32768)
                       for i in range(2)]
                pcq = [E2(nc.psum_tensor(f"pcqb{i}", [128, 1024], F32)) for i in range(2)]

                if True:
                    blk = blk_mid
                    wdq_dma = {}
                    cq_mm = {}
                    cq_ev = {}

                    def load_strip(st):
                        @blk.gpsimd
                        def _(g, st=st):
                            if st >= 2:
                                W.wait(g, e_t, cq_mm[st - 2])
                            g.dma_start(out=wdq[st % 2][:, :],
                                        in_=w_dq[st, :, :]).then_inc(gw[st % 2].h, 16)
                            gw[st % 2].v += 16
                            wdq_dma[st] = (st % 2, gw[st % 2].v)

                    load_strip(0)
                    load_strip(1)
                    for st in range(3):
                        for j in range(4):
                            c1c = st * 4 + j

                            @blk.tensor
                            def _(t, st=st, j=j, c1c=c1c):
                                W.wait(t, gw[wdq_dma[st][0]], wdq_dma[st][1])
                                W.wait(t, e_s, cq_ev.get(c1c - 2, 0))
                                pb = pcq[c1c % 2]
                                for dc in range(NDC):
                                    for hf in range(2):
                                        ins = t.matmul(
                                            pb[:, hf * 512:hf * 512 + 512],
                                            wdq[st % 2][:, dc * 512 + j * 128:
                                                        dc * 512 + j * 128 + 128],
                                            xnq[:, dc * QT + hf * 512:
                                                dc * QT + hf * 512 + 512],
                                            start=(dc == 0), stop=(dc == NDC - 1))
                                cq_mm[st] = e_t.inc(ins)

                            @blk.scalar
                            def _(sc, c1c=c1c):
                                W.wait(sc, d_sy, p0_sy)
                                W.wait(sc, e_t, cq_mm[st])
                                ins = sc.activation(CQ[:, c1c * QT:(c1c + 1) * QT],
                                                    pcq[c1c % 2][:, :], AF.Identity,
                                                    bias=b_dq_t[:, c1c:c1c + 1])
                                cq_ev[c1c] = e_s.inc(ins)
                        if st + 2 < 3:
                            load_strip(st + 2)

                    if t_cq is not None:
                        @blk.sync
                        def _(sy):
                            W.wait(sy, e_s)
                            sy.dma_start(out=t_cq, in_=CQ[:, :]).then_inc(d_sy.h, 16)
                            d_sy.v += 16

            if upto == "p1b_cq":
                blk_mid.__exit__(None, None, None)
                return nc, tap_aps

            with ExitStack() as ph2:
                E2 = ph2.enter_context
                wuq = [sb(f"wuq{i}", [128, 12 * 512], BF16, 149760 + i * 12288)
                       for i in range(2)]
                pq = [E2(nc.psum_tensor(f"pqb{i}", [128, 1024], F32)) for i in range(2)]

                if True:
                    blk = blk_mid
                    q_dma = {}
                    q_mm = {}
                    q_ev = {}

                    def load_qstrip(si):
                        @blk.gpsimd
                        def _(g, si=si):
                            if si >= 2:
                                W.wait(g, e_t, q_mm[si - 2])
                            else:
                                W.wait(g, e_t, cq_mm[2])
                            src = w_uq[si, :, :] if si < 8 else w_rq[si - 8, :, :]
                            g.dma_start(out=wuq[si % 2][:, :],
                                        in_=src).then_inc(gw[si % 2].h, 16)
                            gw[si % 2].v += 16
                            q_dma[si] = (si % 2, gw[si % 2].v)

                    load_qstrip(0)
                    load_qstrip(1)
                    nchunk = 0
                    for si in range(12):
                        for j in range(4):
                            @blk.tensor
                            def _(t, si=si, j=j, nchunk=nchunk):
                                W.wait(t, gw[q_dma[si][0]], q_dma[si][1])
                                if nchunk == 0:
                                    W.wait(t, e_s, cq_ev[11])
                                W.wait(t, e_s, q_ev.get(nchunk - 2, 0))
                                pb = pq[nchunk % 2]
                                for cc in range(12):
                                    for hf in range(2):
                                        ins = t.matmul(
                                            pb[:, hf * 512:hf * 512 + 512],
                                            wuq[si % 2][:, cc * 512 + j * 128:
                                                        cc * 512 + j * 128 + 128],
                                            CQ[:, cc * QT + hf * 512:
                                               cc * QT + hf * 512 + 512],
                                            start=(cc == 0), stop=(cc == 11))
                                q_mm[si] = e_t.inc(ins)

                            @blk.scalar
                            def _(sc, si=si, j=j, nchunk=nchunk):
                                W.wait(sc, d_st)
                                W.wait(sc, e_t, q_mm[si])
                                if si < 8:
                                    hdc = si * 4 + j
                                    ins = sc.activation(QT_[:, hdc * QT:(hdc + 1) * QT],
                                                        pq[nchunk % 2][:, :], AF.Identity,
                                                        bias=b_uq_t[:, hdc:hdc + 1])
                                else:
                                    rc = (si - 8) * 4 + j
                                    ins = sc.activation(QrT[:, rc * QT:(rc + 1) * QT],
                                                        pq[nchunk % 2][:, :], AF.Identity,
                                                        bias=b_rq_t[:, rc:rc + 1])
                                q_ev[nchunk] = e_s.inc(ins)
                            nchunk += 1
                        if si + 2 < 12:
                            load_qstrip(si + 2)

                    @blk.vector
                    def _(v):
                        W.wait(v, e_s)
                        for h in range(H):
                            ins = v.reduce_max(aQ[:, h:h + 1], QT_[:, h * QT:(h + 1) * QT],
                                               axis=AX.X, apply_absolute_value=True)
                        for rc in range(16):
                            ins = v.reduce_max(aQr[:, rc:rc + 1],
                                               QrT[:, rc * QT:(rc + 1) * QT],
                                               axis=AX.X, apply_absolute_value=True)
                        e_v.inc(ins)

                    if t_qt is not None:
                        @blk.sync
                        def _(sy):
                            W.wait(sy, e_s)
                            sy.dma_start(out=t_qt, in_=QT_[:, :]).then_inc(d_sy.h, 16)
                            sy.dma_start(out=t_qrt, in_=QrT[:, :]).then_inc(d_sy.h, 16)
                            d_sy.v += 32

        if upto == "p1b_q":
            blk_mid.__exit__(None, None, None)
            return nc, tap_aps

        with ExitStack() as ph:
            E1 = ph.enter_context
            wkv = sb("wkv", [128, 4 * D], BF16, 125184)
            kev = [sb(f"kev{i}", [128, 2048], BF16, 157952 + i * 4096) for i in range(2)]

            if True:
                blk = blk_mid
                @blk.gpsimd
                def _(g):
                    W.wait(g, e_t, q_mm[11])
                    g.dma_start(out=wkv[:, :], in_=w_uk[:, :]).then_inc(d_gp.h, 16)
                    d_gp.v += 16

                with ExitStack() as ph2:
                    pk = [ph2.enter_context(nc.psum_tensor(f"pk{i}", [128, 2048], F32))
                          for i in range(2)]
                    k_ev = {}
                    k_red = {}
                    k_st = {}
                    for hdc in range(H):
                        @blk.tensor
                        def _(t, hdc=hdc):
                            W.wait(t, d_gp)
                            if hdc == 0:
                                W.wait(t, e_s, q_ev[47])
                            for half in range(2):
                                W.wait(t, e_s, k_ev.get((hdc - 1, half), 0))
                                for cc in range(4):
                                    for tq in range(4):
                                        ins = t.matmul(
                                            pk[half][:, tq * 512:(tq + 1) * 512],
                                            wkv[:, cc * D + hdc * 128:
                                                cc * D + hdc * 128 + 128],
                                            CKV[:, cc * S + half * 2048 + tq * 512:
                                                cc * S + half * 2048 + tq * 512 + 512],
                                            start=(cc == 0), stop=(cc == 3))
                                e_t.inc(ins)

                        @blk.scalar
                        def _(sc, hdc=hdc):
                            W.wait(sc, d_sy, p0_sy)
                            base = e_t.v - 2
                            for half in range(2):
                                W.wait(sc, e_t, base + half + 1)
                                W.wait(sc, e_v, k_red.get((hdc - 1, half), 0))
                                W.wait(sc, d_st, k_st.get(hdc - 1, 0))
                                ins = sc.activation(kev[half][:, :], pk[half][:, :],
                                                    AF.Identity,
                                                    bias=b_uk_t[:, hdc:hdc + 1])
                                k_ev[(hdc, half)] = e_s.inc(ins)

                        @blk.vector
                        def _(v, hdc=hdc):
                            W.wait(v, e_s, k_ev[(hdc, 0)])
                            ins = v.reduce_max(aK[:, hdc:hdc + 1], kev[0][:, :],
                                               axis=AX.X, apply_absolute_value=True)
                            k_red[(hdc, 0)] = e_v.inc(ins)
                            W.wait(v, e_s, k_ev[(hdc, 1)])
                            W.wait(v, e_v)
                            ins = v.reduce_max(cvec[:, 3:4], kev[1][:, :],
                                               axis=AX.X, apply_absolute_value=True)
                            k_red[(hdc, 1)] = e_v.inc(ins)
                            W.wait(v, e_v)
                            ins = v.tensor_tensor(out=aK[:, hdc:hdc + 1],
                                                  in0=aK[:, hdc:hdc + 1],
                                                  in1=cvec[:, 3:4], op=ALU.max)
                            e_v.inc(ins)

                        @blk.sync
                        def _(sy, hdc=hdc):
                            W.wait(sy, e_s, k_ev[(hdc, 1)])
                            sy.dma_start(out=sc_KT[hdc * 128:(hdc + 1) * 128, 0:2048],
                                         in_=kev[0][:, :]).then_inc(d_st.h, 16)
                            sy.dma_start(out=sc_KT[hdc * 128:(hdc + 1) * 128, 2048:4096],
                                         in_=kev[1][:, :]).then_inc(d_st.h, 16)
                            d_st.v += 32
                            k_st[hdc] = d_st.v

                    @blk.vector
                    def _(v):
                        ins = v.reduce_max(aKr[0:64, 0:1], Kr2[0:64, :], axis=AX.X,
                                           apply_absolute_value=True)
                        e_v.inc(ins)

                    @blk.sync
                    def _(sy):
                        W.wait(sy, e_v)
                        sy.dma_start(out=aKr[64:128, :], in_=aKr[0:64, :]
                                     ).then_inc(d_sy.h, 16)
                        d_sy.v += 16

            if True:
                blk = blk_mid
                @blk.gpsimd
                def _(g):
                    W.wait(g, e_t)
                    g.dma_start(out=wkv[:, :], in_=w_uv[:, :]).then_inc(d_gp.h, 16)
                    d_gp.v += 16

                with ExitStack() as ph2:
                    pvb = ph2.enter_context(nc.psum_tensor("pvb", [128, NDC], F32))

                    @blk.tensor
                    def _(t):
                        W.wait(t, d_gp)
                        W.wait(t, e_s, k_ev[(31, 1)])
                        for hdc in range(NDC):
                            for cc in range(4):
                                ins = t.matmul(pvb[:, hdc:hdc + 1],
                                               wkv[:, cc * D + hdc * 128:
                                                   cc * D + hdc * 128 + 128],
                                               b_dkv_t[:, cc:cc + 1],
                                               start=(cc == 0), stop=(cc == 3))
                        e_t.inc(ins)

                    @blk.vector
                    def _(v):
                        W.wait(v, e_t)
                        ins = v.tensor_tensor(out=vbias[:, :], in0=vbias[:, :],
                                              in1=pvb[:, :], op=ALU.add)
                        e_v.inc(ins)

                with ExitStack() as ph2:
                    pv = [ph2.enter_context(nc.psum_tensor(f"pv{i}", [128, 2048], F32))
                          for i in range(2)]
                    v_ev = {}
                    v_st = {}
                    for ts in range(32):
                        @blk.tensor
                        def _(t, ts=ts):
                            if ts == 0:
                                W.wait(t, e_v)
                            for half in range(2):
                                W.wait(t, e_s, v_ev.get((ts - 1, half), 0))
                                for cc in range(4):
                                    for dq in range(4):
                                        ins = t.matmul(
                                            pv[half][:, dq * 512:(dq + 1) * 512],
                                            CKV[:, cc * S + ts * 128:
                                                cc * S + ts * 128 + 128],
                                            wkv[:, cc * D + half * 2048 + dq * 512:
                                                cc * D + half * 2048 + dq * 512 + 512],
                                            start=(cc == 0), stop=(cc == 3))
                                e_t.inc(ins)

                        @blk.scalar
                        def _(sc, ts=ts):
                            base = e_t.v - 2
                            for half in range(2):
                                W.wait(sc, e_t, base + half + 1)
                                if ts == 0:
                                    W.wait(sc, d_st)
                                W.wait(sc, d_st, v_st.get(ts - 1, 0))
                                ins = sc.activation(kev[half][:, :], pv[half][:, :],
                                                    AF.Identity)
                                v_ev[(ts, half)] = e_s.inc(ins)

                        @blk.sync
                        def _(sy, ts=ts):
                            W.wait(sy, e_s, v_ev[(ts, 1)])
                            sy.dma_start(out=sc_V[ts * 128:(ts + 1) * 128, 0:2048],
                                         in_=kev[0][:, :]).then_inc(d_st.h, 16)
                            sy.dma_start(out=sc_V[ts * 128:(ts + 1) * 128, 2048:4096],
                                         in_=kev[1][:, :]).then_inc(d_st.h, 16)
                            d_st.v += 32
                            v_st[ts] = d_st.v

                if t_ab is not None:
                    @blk.scalar
                    def _(sc):
                        W.wait(sc, e_v)
                        sc.dma_start(out=t_ab[:, 0:NDC], in_=aK[:, :]).then_inc(d_sc.h, 16)
                        sc.dma_start(out=t_ab[:, NDC:2 * NDC], in_=vbias[:, :]
                                     ).then_inc(d_sc.h, 16)
                        d_sc.v += 32

        blk_mid.__exit__(None, None, None)

        if upto == "p2":
            return nc, tap_aps

        attnT = sb("attnT", [128, H * QT], BF16, 125184)

        with ExitStack() as ph:
            E1 = ph.enter_context
            kt = [sb(f"kt{i}", [128, 2048], BF16, 26656 + i * 4096) for i in range(2)]
            vh = [sb(f"vh{i}", [128, 2048], BF16, 34848 + i * 4096) for i in range(2)]
            PT = [sb(f"PT{i}", [128, QT], BF16,
                     (190720, 192768, 47136)[i]) for i in range(3)]
            rz = sb("rz", [128, QT], F32, 43040)
            Zacc = sb("Zacc", [128, QT], F32, 49184)
            qr_cur = [sb(f"qrc{i}", [128, QT], BF16, 53280 + i * 2048) for i in range(2)]

            with ExitStack() as ph2:
                E2 = ph2.enter_context
                pU = E2(nc.psum_tensor("pU", [1, H], F32))
                pU2 = E2(nc.psum_tensor("pU2", [128, H], F32))
                with nc.Block("p3u") as blk:
                    @blk.tensor
                    def _(t):
                        W.wait(t, d_sy)
                        for h in range(H):
                            t.matmul(pU[0:1, h:h + 1], aQ[:, h:h + 1], aK[:, h:h + 1],
                                     start=True, stop=False)
                            ins = t.matmul(pU[0:1, h:h + 1],
                                           aQr[(h % 2) * 64:(h % 2) * 64 + 64,
                                               h // 2:h // 2 + 1],
                                           aKr[(h % 2) * 64:(h % 2) * 64 + 64, 0:1],
                                           start=False, stop=True)
                        e_t.inc(ins)

                    @blk.vector
                    def _(v):
                        W.wait(v, e_t)
                        ins = v.tensor_scalar(out=nurow[0:1, :], in0=pU[0:1, :],
                                              scalar1=-SCALER, scalar2=None, op0=ALU.mult)
                        e_v.inc(ins)

                    @blk.tensor
                    def _(t):
                        W.wait(t, e_v)
                        ins = t.matmul(pU2[:, :], ones128[0:1, :], nurow[0:1, :],
                                       start=True, stop=True)
                        e_t.inc(ins)

                    @blk.vector
                    def _(v):
                        W.wait(v, e_t)
                        ins = v.tensor_copy(out=negU[:, :], in_=pU2[:, :])
                        e_v.inc(ins)

                    if t_negu is not None:
                        @blk.sync
                        def _(sy):
                            W.wait(sy, e_v)
                            sy.dma_start(out=t_negu, in_=negU[:, :]).then_inc(d_sy.h, 16)
                            d_sy.v += 16

            with ExitStack() as ph2:
                E2 = ph2.enter_context
                pS3 = [E2(nc.psum_tensor(f"pS{i}", [128, QT], F32)) for i in range(3)]
                pPV = E2(nc.psum_tensor("pPV", [128, QT], F32))

                with nc.Block("p3") as blk:
                    kv_dma = {}
                    qr_dma = {}
                    pe_head_done = {}
                    rot = 0
                    prev_exp29 = prev_exp31 = 0
                    attn_evac = {}
                    rz_done = {}
                    prev_head = None

                    for h in range(H):
                        qb = h % 2

                        @blk.sync
                        def _(sy, h=h, qb=qb):
                            if h == 0:
                                W.wait(sy, d_st)
                            if h > 0:
                                W.wait(sy, e_t, prev_head["zpv"](15))
                            sy.dma_start(out=kt[0][:, :],
                                         in_=sc_KT[h * 128:(h + 1) * 128, 0:2048]
                                         ).then_inc(kv[0].h, 16)
                            sy.dma_start(out=vh[0].rearrange("p (ks d) -> p ks d", ks=16),
                                         in_=sc_V.rearrange("(ks p) o -> p ks o", p=128)
                                         [:, 0:16, h * 128:(h + 1) * 128]
                                         ).then_inc(kv[0].h, 16)
                            kv[0].v += 32
                            kv_dma[(h, 0)] = kv[0].v
                            src = QrT[(h % 2) * 64:(h % 2) * 64 + 64,
                                      (h // 2) * QT:(h // 2) * QT + QT]
                            sy.dma_start(out=qr_cur[qb][0:64, :], in_=src
                                         ).then_inc(qr.h, 16)
                            sy.dma_start(out=qr_cur[qb][64:128, :], in_=src
                                         ).then_inc(qr.h, 16)
                            qr.v += 32
                            qr_dma[h] = qr.v
                            if h > 0:
                                W.wait(sy, e_t, prev_head["zpv"](31))
                            sy.dma_start(out=kt[1][:, :],
                                         in_=sc_KT[h * 128:(h + 1) * 128, 2048:4096]
                                         ).then_inc(kv[1].h, 16)
                            sy.dma_start(out=vh[1].rearrange("p (ks d) -> p ks d", ks=16),
                                         in_=sc_V.rearrange("(ks p) o -> p ks o", p=128)
                                         [:, 16:32, h * 128:(h + 1) * 128]
                                         ).then_inc(kv[1].h, 16)
                            kv[1].v += 32
                            kv_dma[(h, 1)] = kv[1].v

                        t0 = e_t.v
                        s0 = e_s.v
                        g0 = e_g.v
                        v0 = e_v.v
                        r = rot
                        B = lambda ks, r=r: (ks + r) % 3
                        bb = B(30)
                        exp_done = lambda ks, s0=s0: s0 + ks + 1
                        vadd_done = (lambda ks, g0=g0, v0=v0:
                                     (v0 if ks % 2 == 0 else g0) + ks // 2 + 1)
                        pair_done = lambda p, t0=t0: t0 + (p + 1 if p < 2 else 3 * p - 1)
                        zpv_done = (lambda ks, t0=t0:
                                    t0 + (3 + 3 * (ks // 2) + ks % 2 if ks <= 27
                                          else 17 + ks))

                        @blk.tensor
                        def _(t, h=h, qb=qb, t0=t0, s0=s0):
                            def kpass(ks):
                                kh, kc = ks // 16, (ks % 16) * 128
                                pS = pS3[B(ks)]
                                W.wait(t, kv[kh], kv_dma[(h, kh)])
                                if ks < 3:
                                    if h == 0:
                                        W.wait(t, d_sy)
                                    elif ks == 0:
                                        W.wait(t, e_s, prev_exp29)
                                    elif ks == 2:
                                        W.wait(t, e_s, prev_exp31)
                                    else:
                                        W.wait(t, e_s, rz_done[h - 1])
                                else:
                                    W.wait(t, e_s, exp_done(ks - 3))
                                for hf in range(2):
                                    t.matmul(pS[:, hf * 512:hf * 512 + 512],
                                             kt[kh][:, kc:kc + 128],
                                             QT_[:, h * QT + hf * 512:
                                                 h * QT + hf * 512 + 512],
                                             start=True, stop=False)

                            def spair(p):
                                ks, ks1 = 2 * p, 2 * p + 1
                                kpass(ks)
                                kpass(ks1)
                                W.wait(t, qr, qr_dma[h])
                                for hf in range(2):
                                    t.matmul(pS3[B(ks)][:, hf * 512:hf * 512 + 512],
                                             Kr2[0:64, ks * 128:(ks + 1) * 128],
                                             qr_cur[qb][0:64, hf * 512:hf * 512 + 512],
                                             start=False, stop=True)
                                    ins = t.matmul(
                                        pS3[B(ks1)][:, hf * 512:hf * 512 + 512],
                                        Kr2[64:128, ks1 * 128:(ks1 + 1) * 128],
                                        qr_cur[qb][64:128, hf * 512:hf * 512 + 512],
                                        start=False, stop=True)
                                e_t.inc(ins)

                            def pv(ks):
                                kh, kc = ks // 16, (ks % 16) * 128
                                W.wait(t, e_s, exp_done(ks))
                                if ks == 0:
                                    W.wait(t, e_v, attn_evac.get(h - 1, 0))
                                for hf in range(2):
                                    ins = t.matmul(pPV[:, hf * 512:hf * 512 + 512],
                                                   vh[kh][:, kc:kc + 128],
                                                   PT[B(ks)][:, hf * 512:hf * 512 + 512],
                                                   start=(ks == 0), stop=(ks == 31))
                                e_t.inc(ins)

                            spair(0)
                            spair(1)
                            for p in range(14):
                                pv(2 * p)
                                pv(2 * p + 1)
                                spair(p + 2)
                            pv(28)
                            pv(29)
                            pv(30)
                            pv(31)
                            W.wait(t, e_v, v0 + 16)
                            W.wait(t, e_s, exp_done(31))
                            for hf in range(2):
                                for j in (29, 30, 31):
                                    ins = t.matmul(pS3[bb][:, hf * 512:hf * 512 + 512],
                                                   ones128[:, :],
                                                   PT[B(j)][:, hf * 512:hf * 512 + 512],
                                                   start=(j == 29), stop=(j == 31))
                            e_t.inc(ins)

                        @blk.scalar
                        def _(sc, h=h, t0=t0):
                            for ks in range(32):
                                W.wait(sc, e_t, pair_done(ks // 2))
                                if ks >= 3:
                                    W.wait(sc, e_t, zpv_done(ks - 3))
                                    W.wait(sc, e_v if (ks - 3) % 2 == 0 else e_g,
                                           vadd_done(ks - 3))
                                elif h > 0:
                                    W.wait(sc, e_v, attn_evac[h - 1])
                                sc.activation(PT[B(ks)][:, 0:512],
                                              pS3[B(ks)][:, 0:512], AF.Exp,
                                              scale=SCALER, bias=negU[:, h:h + 1])
                                ins = sc.activation(PT[B(ks)][:, 512:1024],
                                                    pS3[B(ks)][:, 512:1024], AF.Exp,
                                                    scale=SCALER, bias=negU[:, h:h + 1])
                                e_s.inc(ins)
                            W.wait(sc, e_t, t0 + 49)
                            ins = sc.activation(rz[:, :], pS3[bb][:, :], AF.Ln)
                            m = e_s.inc(ins)
                            W.wait(sc, e_s, m)
                            ins = sc.activation(rz[:, :], rz[:, :], AF.Exp, scale=-1.0)
                            rz_done[h] = e_s.inc(ins)

                        @blk.gpsimd
                        def _(g, h=h, s0=s0):
                            for ks in range(1, 30, 2):
                                W.wait(g, e_s, exp_done(ks))
                                W.wait(g, e_g)
                                if ks == 1:
                                    ins = g.tensor_copy(out=rz[:, :], in_=PT[B(1)][:, :])
                                else:
                                    ins = g.tensor_tensor(out=rz[:, :], in0=rz[:, :],
                                                          in1=PT[B(ks)][:, :], op=ALU.add)
                                e_g.inc(ins)

                        @blk.vector
                        def _(v, h=h, s0=s0, g0=g0, t0=t0):
                            for ks in range(0, 30, 2):
                                W.wait(v, e_s, exp_done(ks))
                                W.wait(v, e_v)
                                if ks == 0:
                                    ins = v.tensor_copy(out=Zacc[:, :], in_=PT[B(0)][:, :])
                                else:
                                    ins = v.tensor_tensor(out=Zacc[:, :], in0=Zacc[:, :],
                                                          in1=PT[B(ks)][:, :], op=ALU.add)
                                e_v.inc(ins)
                            W.wait(v, e_g, g0 + 15)
                            W.wait(v, e_v)
                            W.wait(v, e_t, zpv_done(29))
                            ins = v.tensor_tensor(out=PT[B(29)][:, :], in0=Zacc[:, :],
                                                  in1=rz[:, :], op=ALU.add)
                            e_v.inc(ins)

                        assert e_t.v == t0 + 49 and e_s.v == s0 + 34
                        assert e_g.v == g0 + 15 and e_v.v == v0 + 16
                        prev_head = {"exp": exp_done, "zpv": zpv_done}
                        pe_head_done[h] = zpv_done(31)
                        prev_exp29 = exp_done(29)
                        prev_exp31 = exp_done(31)
                        rot = (rot + 2) % 3

                        @blk.vector
                        def _(v, h=h):
                            W.wait(v, e_s, rz_done[h])
                            ins = v.tensor_tensor(out=attnT[:, h * QT:(h + 1) * QT],
                                                  in0=pPV[:, :], in1=rz[:, :],
                                                  op=ALU.mult)
                            m = e_v.inc(ins)
                            W.wait(v, e_v, m)
                            ins = v.tensor_scalar(out=attnT[:, h * QT:(h + 1) * QT],
                                                  in0=attnT[:, h * QT:(h + 1) * QT],
                                                  scalar1=vbias[:, h:h + 1],
                                                  scalar2=None, op0=ALU.add)
                            attn_evac[h] = e_v.inc(ins)

                    if t_attn is not None:
                        @blk.sync
                        def _(sy):
                            W.wait(sy, e_v)
                            sy.dma_start(out=t_attn, in_=attnT[:, :]).then_inc(d_sy.h, 16)
                            d_sy.v += 16

        if upto == "p3":
            return nc, tap_aps

        with ExitStack() as ph:
            E1 = ph.enter_context
            xnT2 = sb("xnT2", [128, NDC * QT], BF16, 59648)
            at = sb("at", [128, NDC * 256], BF16, 26656)
            bft = sb("bft", [128, 2 * D], BF16, 43040)
            xaT = sb("xaT", [128, 2 * QT], BF16, 207104)
            wo = [sb(f"wo{i}", [128, NDC * 128], BF16, 26656 + i * 8192) for i in range(2)]
            wf = [sb(f"wf{i}", [128, NDC * 128], BF16, 190720 + i * 8192) for i in range(2)]
            xr = [sb(f"xr{i}", [128, QT], F32, 18464 + i * 4096) for i in range(2)]
            ob = [sb("ob", [128, QT], F32, 211200)] * 2

            with nc.Block("p4") as blk:
                @blk.sync
                def _(sy):
                    W.wait(sy, d_st)
                    sy.dma_start(out=xnT2.rearrange("p (dc q) -> p dc q", dc=NDC),
                                 in_=sc_xnT.rearrange("(dc p) q -> p dc q", p=128)
                                 ).then_inc(d_sy.h, 16)
                    d_sy.v += 16

                @blk.gpsimd
                def _(g):
                    g.dma_start(out=at[:, :], in_=A_T[:, :]).then_inc(d_gp.h, 16)
                    g.dma_start(out=bft[:, :], in_=Bf_T[:, :]).then_inc(d_gp.h, 16)
                    d_gp.v += 32

                with ExitStack() as ph2:
                    pxa = ph2.enter_context(nc.psum_tensor("pxa", [128, QT], F32))
                    xa_ev = {}
                    for erc in range(2):
                        @blk.tensor
                        def _(t, erc=erc):
                            W.wait(t, d_gp)
                            W.wait(t, d_sy)
                            if erc == 1:
                                W.wait(t, e_v, xa_ev[0])
                            for dc in range(NDC):
                                for hf in range(2):
                                    ins = t.matmul(
                                        pxa[:, hf * 512:hf * 512 + 512],
                                        at[:, dc * 256 + erc * 128:
                                           dc * 256 + erc * 128 + 128],
                                        xnT2[:, dc * QT + hf * 512:
                                             dc * QT + hf * 512 + 512],
                                        start=(dc == 0), stop=(dc == NDC - 1))
                            e_t.inc(ins)

                        @blk.vector
                        def _(v, erc=erc):
                            W.wait(v, e_t)
                            ins = v.tensor_scalar(out=xaT[:, erc * QT:(erc + 1) * QT],
                                                  in0=pxa[:, :],
                                                  scalar1=wscale[:, erc:erc + 1],
                                                  scalar2=None, op0=ALU.mult)
                            xa_ev[erc] = e_v.inc(ins)
                    if t_xa is not None:
                        @blk.sync
                        def _(sy):
                            W.wait(sy, e_v)
                            sy.dma_start(out=t_xa, in_=xaT[:, :]).then_inc(d_sy.h, 16)
                            d_sy.v += 16

                with ExitStack() as ph2:
                    po = [ph2.enter_context(nc.psum_tensor(f"po{i}", [128, QT], F32))
                          for i in range(2)]
                    wdma = {}
                    xdma = {}
                    mm_done = {}
                    ev_done = {}
                    st_done = {}
                    for oc in range(NDC):
                        @blk.gpsimd
                        def _(g, oc=oc):
                            if oc >= 2:
                                W.wait(g, e_t, mm_done[oc - 2])
                            else:
                                W.wait(g, e_t)
                            g.dma_start(out=wo[oc % 2][:, :],
                                        in_=w_out[oc, :, :]).then_inc(gw[oc % 2].h, 16)
                            g.dma_start(out=wf[oc % 2][:, :],
                                        in_=WT[oc, :, :]).then_inc(gw[oc % 2].h, 16)
                            gw[oc % 2].v += 32
                            wdma[oc] = gw[oc % 2].v

                        @blk.sync
                        def _(sy, oc=oc):
                            if oc >= 2:
                                W.wait(sy, e_v, ev_done[oc - 2])
                            sy.dma_start(out=xr[oc % 2][:, :],
                                         in_=xTq[oc * 128:(oc + 1) * 128, :]
                                         ).then_inc(xs[oc % 2].h, 16)
                            xs[oc % 2].v += 16
                            xdma[oc] = xs[oc % 2].v

                        @blk.tensor
                        def _(t, oc=oc):
                            W.wait(t, gw[oc % 2], wdma[oc])
                            if oc >= 2:
                                W.wait(t, e_v, ev_done[oc - 2])
                            else:
                                W.wait(t, e_v, xa_ev[1])
                            pb = po[oc % 2]
                            for dc in range(NDC):
                                for hf in range(2):
                                    t.matmul(pb[:, hf * 512:hf * 512 + 512],
                                             wo[oc % 2][:, dc * 128:(dc + 1) * 128],
                                             attnT[:, dc * QT + hf * 512:
                                                   dc * QT + hf * 512 + 512],
                                             start=(dc == 0), stop=False)
                            for dc in range(NDC):
                                for hf in range(2):
                                    t.matmul(pb[:, hf * 512:hf * 512 + 512],
                                             wf[oc % 2][:, dc * 128:(dc + 1) * 128],
                                             xnT2[:, dc * QT + hf * 512:
                                                  dc * QT + hf * 512 + 512],
                                             start=False, stop=False)
                            for erc in range(2):
                                for hf in range(2):
                                    ins = t.matmul(
                                        pb[:, hf * 512:hf * 512 + 512],
                                        bft[:, erc * D + oc * 128:
                                            erc * D + oc * 128 + 128],
                                        xaT[:, erc * QT + hf * 512:
                                            erc * QT + hf * 512 + 512],
                                        start=False, stop=(erc == 1))
                            mm_done[oc] = e_t.inc(ins)

                        @blk.vector
                        def _(v, oc=oc):
                            W.wait(v, e_t, mm_done[oc])
                            W.wait(v, xs[oc % 2], xdma[oc])
                            if oc >= 1:
                                W.wait(v, d_sc, st_done[oc - 1])
                            ins = v.scalar_tensor_tensor(out=ob[oc % 2][:, :],
                                                         in0=po[oc % 2][:, :],
                                                         scalar=bsum[:, oc:oc + 1],
                                                         in1=xr[oc % 2][:, :],
                                                         op0=ALU.add, op1=ALU.add)
                            ev_done[oc] = e_v.inc(ins)

                        @blk.scalar
                        def _(sc, oc=oc):
                            W.wait(sc, e_v, ev_done[oc])
                            sc.dma_start(out=outT[oc * 128:(oc + 1) * 128, :],
                                         in_=ob[oc % 2][:, :]).then_inc(d_sc.h, 16)
                            d_sc.v += 16
                            st_done[oc] = d_sc.v

                    @blk.scalar
                    def _(sc):
                        W.wait(sc, d_sc)

    return nc, tap_aps


_NC_CACHE = {}


def get_program(taps=(), upto="p4"):
    key = (tuple(sorted(taps)), upto)
    if key not in _NC_CACHE:
        _NC_CACHE[key] = build_program(taps, upto)
    return _NC_CACHE[key]


def make_in_maps(inputs):
    x = np.asarray(inputs["L0_attn_sequence"], dtype=np.float32)
    c = np.ascontiguousarray
    f32 = lambda k: np.asarray(inputs[k], np.float32)

    def colsT(b, nc_):
        return c(f32(b).reshape(nc_, 128).T)

    def lhsT_tiles(w, strips, swidth):
        K, M = w.shape
        return c(w.reshape(K // 128, 128, strips, swidth)
                 .transpose(2, 1, 0, 3).reshape(strips, 128, (K // 128) * swidth))

    shared = dict(
        w_dq=lhsT_tiles(f32("L0_attn_w_dq"), 3, 512),
        b_dq=colsT("L0_attn_b_dq", 12),
        w_uq=lhsT_tiles(f32("L0_attn_w_uq"), 8, 512),
        b_uq=colsT("L0_attn_b_uq", NDC),
        w_dkv=lhsT_tiles(f32("L0_attn_w_dkv"), 1, DC)[0],
        b_dkv=colsT("L0_attn_b_dkv", 4),
        w_uk=lhsT_tiles(f32("L0_attn_w_uk"), 1, D)[0],
        b_uk=colsT("L0_attn_b_uk", NDC),
        w_uv=lhsT_tiles(f32("L0_attn_w_uv"), 1, D)[0],
        b_uv=colsT("L0_attn_b_uv", NDC),
        w_rq=lhsT_tiles(f32("L0_attn_w_rq"), 4, 512),
        b_rq=colsT("L0_attn_b_rq", 16),
        w_rk=lhsT_tiles(f32("L0_attn_w_rk"), 1, DR)[0],
        b_rk=c(f32("L0_attn_b_rk").reshape(64, 1)),
        w_out=lhsT_tiles(f32("L0_attn_w_out"), NDC, 128),
        b_out=colsT("L0_attn_b_out", NDC),
        WT=lhsT_tiles(c(f32("L0_ffn_W").T), NDC, 128),
        fbias=colsT("L0_ffn_bias", NDC),
        A_T=lhsT_tiles(c(f32("L0_ffn_A").reshape(256, D).T), 1, 256)[0],
        Bf_T=lhsT_tiles(f32("L0_ffn_Bf").transpose(0, 2, 1).reshape(256, D), 1, D)[0],
        cs_rep=c(np.repeat(np.asarray(inputs["L0_ffn_coeffs"], np.float32), 16)
                 .reshape(2, 128).T),
        sc_rep=c(np.repeat(np.asarray(inputs["L0_ffn_scales"], np.float32), 16)
                 .reshape(2, 128).T),
        ident=np.eye(128, dtype=ml_dtypes.bfloat16),
    )
    in_maps = []
    for core in range(8):
        b, qs = core // 4, (core % 4) * QT
        m = dict(shared)
        m["xb"] = c(x[b])
        m["xq"] = c(x[b, qs:qs + QT, :])
        m["xTq"] = c(x[b, qs:qs + QT, :].T)
        in_maps.append(m)
    return in_maps


def kernel(**inputs):
    nc, _ = get_program()
    in_maps = make_in_maps(inputs)
    res = run_bass_kernel_spmd(nc, in_maps, core_ids=list(range(8)))
    out = np.empty((B, S, D), dtype=np.float32)
    for core in range(8):
        b, qs = core // 4, (core % 4) * QT
        out[b, qs:qs + QT, :] = res.results[core]["outT"].T
    return out


# version 21
# speedup vs baseline: 1.0240x; 1.0240x over previous
import numpy as np
import ml_dtypes
from contextlib import ExitStack

import concourse.bass as bass
import concourse.mybir as mybir
from concourse.bass_utils import run_bass_kernel_spmd

F32 = mybir.dt.float32
BF16 = mybir.dt.bfloat16
AF = mybir.ActivationFunctionType
AX = mybir.AxisListType
ALU = mybir.AluOpType

B, S, D = 2, 4096, 4096
H, DH, DR = 32, 128, 64
DC, DC1 = 512, 1536
QT = 1024
SCALER = 1.0 / float(np.sqrt(DH + DR))
EPS = 1e-5
NDC = D // 128


class Sem:
    def __init__(self, ctx, nc, name):
        self.h = ctx.enter_context(nc.semaphore(name))
        self.v = 0

    def inc(self, instr, n=1):
        instr.then_inc(self.h, n)
        self.v += n
        return self.v


class Waiter:

    def __init__(self):
        self.marks = {}

    def wait(self, eng, sem, val=None):
        v = sem.v if val is None else val
        if v <= 0:
            return
        key = (str(getattr(eng, "engine", type(eng).__name__)), sem.h.num)
        if self.marks.get(key, -1) < v:
            eng.wait_ge(sem.h, v)
            self.marks[key] = v


def build_program(taps=(), upto="p4"):
    nc = bass.Bass("TRN2", target_bir_lowering=False)
    taps = set(taps)

    xb = nc.dram_tensor("xb", [S, D], F32, kind="ExternalInput").ap()
    xq = nc.dram_tensor("xq", [QT, D], F32, kind="ExternalInput").ap()
    xTq = nc.dram_tensor("xTq", [D, QT], F32, kind="ExternalInput").ap()
    w_dq = nc.dram_tensor("w_dq", [3, 128, NDC * 512], F32, kind="ExternalInput").ap()
    b_dq = nc.dram_tensor("b_dq", [128, 12], F32, kind="ExternalInput").ap()
    w_uq = nc.dram_tensor("w_uq", [8, 128, 12 * 512], F32, kind="ExternalInput").ap()
    b_uq = nc.dram_tensor("b_uq", [128, NDC], F32, kind="ExternalInput").ap()
    w_dkv = nc.dram_tensor("w_dkv", [128, NDC * DC], F32, kind="ExternalInput").ap()
    b_dkv = nc.dram_tensor("b_dkv", [128, 4], F32, kind="ExternalInput").ap()
    w_uk = nc.dram_tensor("w_uk", [128, 4 * D], F32, kind="ExternalInput").ap()
    b_uk = nc.dram_tensor("b_uk", [128, NDC], F32, kind="ExternalInput").ap()
    w_uv = nc.dram_tensor("w_uv", [128, 4 * D], F32, kind="ExternalInput").ap()
    b_uv = nc.dram_tensor("b_uv", [128, NDC], F32, kind="ExternalInput").ap()
    w_rq = nc.dram_tensor("w_rq", [4, 128, 12 * 512], F32, kind="ExternalInput").ap()
    b_rq = nc.dram_tensor("b_rq", [128, 16], F32, kind="ExternalInput").ap()
    w_rk = nc.dram_tensor("w_rk", [128, NDC * DR], F32, kind="ExternalInput").ap()
    b_rk = nc.dram_tensor("b_rk", [64, 1], F32, kind="ExternalInput").ap()
    w_out = nc.dram_tensor("w_out", [NDC, 128, NDC * 128], F32, kind="ExternalInput").ap()
    b_out = nc.dram_tensor("b_out", [128, NDC], F32, kind="ExternalInput").ap()
    WT = nc.dram_tensor("WT", [NDC, 128, NDC * 128], F32, kind="ExternalInput").ap()
    fbias = nc.dram_tensor("fbias", [128, NDC], F32, kind="ExternalInput").ap()
    A_T = nc.dram_tensor("A_T", [128, NDC * 256], F32, kind="ExternalInput").ap()
    Bf_T = nc.dram_tensor("Bf_T", [128, 2 * D], F32, kind="ExternalInput").ap()
    cs_rep = nc.dram_tensor("cs_rep", [128, 2], F32, kind="ExternalInput").ap()
    sc_rep = nc.dram_tensor("sc_rep", [128, 2], F32, kind="ExternalInput").ap()
    ident = nc.dram_tensor("ident", [128, 128], BF16, kind="ExternalInput").ap()

    outT = nc.dram_tensor("outT", [D, QT], F32, kind="ExternalOutput").ap()

    sc_KT = nc.dram_tensor("sc_KT", [D, S], BF16, kind="Internal").ap()
    sc_V = nc.dram_tensor("sc_V", [S, D], BF16, kind="Internal").ap()
    sc_xnT = nc.dram_tensor("sc_xnT", [D, QT], BF16, kind="Internal").ap()

    tap_aps = {}

    def tap(name, shape, dtype=F32):
        if name in taps:
            tap_aps[name] = nc.dram_tensor("tap_" + name, shape, dtype,
                                           kind="ExternalOutput").ap()
            return tap_aps[name]
        return None

    t_ckv = tap("ckv", [128, 4 * S], BF16)
    t_kr = tap("kr", [128, S], BF16)
    t_xnq = tap("xnq", [128, NDC * QT], BF16)
    t_cq = tap("cq", [128, 12 * QT], BF16)
    t_qt = tap("qt", [128, H * QT], BF16)
    t_qrt = tap("qrt", [128, 16 * QT], BF16)
    t_negu = tap("negu", [128, H])
    t_attn = tap("attn", [128, H * QT], BF16)
    t_xa = tap("xa", [128, 2 * QT], BF16)
    t_ab = tap("ab", [128, 2 * NDC])
    t_lnst = tap("lnst", [128, 16])

    W = Waiter()
    with ExitStack() as top:
        d_sy = Sem(top, nc, "d_sy")
        d_sc = Sem(top, nc, "d_sc")
        d_gp = Sem(top, nc, "d_gp")
        e_t = Sem(top, nc, "e_t")
        e_v = Sem(top, nc, "e_v")
        e_s = Sem(top, nc, "e_s")
        e_g = Sem(top, nc, "e_g")
        d_st = Sem(top, nc, "d_st")
        dx = [Sem(top, nc, f"dx{i}") for i in range(3)]
        kv = [Sem(top, nc, f"kv{i}") for i in range(2)]
        gw = [Sem(top, nc, f"gw{i}") for i in range(2)]
        xs = [Sem(top, nc, f"xs{i}") for i in range(2)]
        qr = Sem(top, nc, "qr")

        _bump = [16640]

        def sb(name, shape, dtype, off=None):
            nbytes = shape[1] * np.dtype(mybir.dt.np(dtype)).itemsize
            if off is None:
                off = (_bump[0] + 31) & ~31
                _bump[0] = off + nbytes
                assert _bump[0] <= 18464, (name, _bump[0])
            return nc.alloc_sbuf_tensor_at(name, shape, dtype, offset=off)

        identb = sb("identb", [128, 128], BF16)
        ones128 = sb("ones128", [128, 128], BF16)
        nurow = sb("nurow", [1, H], BF16)
        cvec = sb("cvec", [128, 4], F32)
        b_dq_t = sb("b_dq_t", [128, 12], F32)
        b_uq_t = sb("b_uq_t", [128, NDC], F32)
        b_rq_t = sb("b_rq_t", [128, 16], F32)
        b_dkv_t = sb("b_dkv_t", [128, 4], BF16)
        b_uk_t = sb("b_uk_t", [128, NDC], F32)
        b_rk_t = sb("b_rk_t", [64, 1], F32)
        vbias = sb("vbias", [128, NDC], F32)
        bsum = sb("bsum", [128, NDC], F32)
        wscale = sb("wscale", [128, 2], F32)
        negU = sb("negU", [128, H], F32)
        aQ = sb("aQ", [128, H], F32)
        aQr = sb("aQr", [128, 16], F32)
        aK = sb("aK", [128, H], F32)
        aKr = sb("aKr", [128, 1], F32)
        Kr2 = sb("Kr2", [128, S], BF16, 18464)

        with nc.Block("p0") as blk:
            @blk.sync
            def _(sy):
                for dst, src in ((b_dq_t, b_dq), (b_uq_t, b_uq), (b_rq_t, b_rq),
                                 (b_uk_t, b_uk), (vbias, b_uv), (bsum, b_out)):
                    sy.dma_start(out=dst[:, :], in_=src[:, :]).then_inc(d_sy.h, 16)
                    d_sy.v += 16
                sy.dma_start(out=identb[:, :], in_=ident[:, :]).then_inc(d_sy.h, 16)
                sy.dma_start(out=b_rk_t[:, :], in_=b_rk[:, :]).then_inc(d_sy.h, 16)
                d_sy.v += 32

            @blk.scalar
            def _(sc):
                sc.dma_start(out=wscale[:, 0:2], in_=cs_rep[:, :]).then_inc(d_sc.h, 16)
                sc.dma_start(out=negU[:, 0:2], in_=sc_rep[:, :]).then_inc(d_sc.h, 16)
                sc.dma_start(out=aK[:, 0:NDC], in_=fbias[:, :]).then_inc(d_sc.h, 16)
                d_sc.v += 48

            @blk.gpsimd
            def _(g):
                g.dma_start(out=b_dkv_t[:, :], in_=b_dkv[:, :]).then_inc(d_gp.h, 16)
                d_gp.v += 16
                g.memset(ones128[:, :], 1.0)
                g.memset(cvec[:, 0:1], EPS)
                g.memset(cvec[:, 1:2], 1.0 / D)
                ins = g.memset(cvec[:, 2:3], -1.0)
                e_g.inc(ins)

            @blk.vector
            def _(v):
                W.wait(v, d_sc)
                W.wait(v, d_sy)
                ins = v.tensor_tensor(out=wscale[:, :], in0=wscale[:, :],
                                      in1=negU[:, 0:2], op=ALU.mult)
                e_v.inc(ins)
                ins = v.tensor_tensor(out=bsum[:, :], in0=bsum[:, :],
                                      in1=aK[:, 0:NDC], op=ALU.add)
                e_v.inc(ins)

        p0_sy = d_sy.v

        def ln_transpose(blk, src_ap, n_tiles, xf, xn_, xnT_of, dump,
                         ssum, ssq, rstd, negmr, ptr, ptr2, pair_consumer):
            tile_ln_done = {}
            tr_mm_done = {}
            grp_evac = {}
            nxf = len(xf)
            s_base = {}
            g_base = {}
            for ti in range(n_tiles):
                tt = ti % 4
                buf = ti % nxf
                nbuf = ti % 2
                s_base[ti] = e_s.v
                g_base[ti] = e_g.v
                s_sq = e_s.v + 1
                s_exp = e_s.v + 3
                g_var = e_g.v + 4
                g_negmr = e_g.v + 6

                @blk.sync
                def _(sy, ti=ti, buf=buf):
                    if ti >= nxf:
                        W.wait(sy, e_s, tile_ln_done[ti - nxf])
                    sy.dma_start(out=xf[buf][:, :],
                                 in_=src_ap[ti * 128:(ti + 1) * 128, :]
                                 ).then_inc(dx[buf].h, 16)
                    dx[buf].v += 16

                @blk.vector
                def _(v, ti=ti, tt=tt, buf=buf):
                    W.wait(v, dx[buf])
                    if ti >= 4:
                        W.wait(v, e_g, g_base[ti - 4])
                    ins = v.reduce_sum(ssum[:, tt:tt + 1], xf[buf][:, :], axis=AX.X)
                    e_v.inc(ins)

                @blk.scalar
                def _(sc, ti=ti, tt=tt, buf=buf, nbuf=nbuf,
                      g_var=g_var, g_negmr=g_negmr):
                    c = slice(tt, tt + 1)
                    W.wait(sc, dx[buf])
                    if ti >= 2:
                        W.wait(sc, e_t, tr_mm_done[ti - 2])
                    if ti >= 4:
                        W.wait(sc, e_g, g_base[ti - 4])
                    ins = sc.activation(xn_[nbuf][:, :], xf[buf][:, :], AF.Square,
                                        accum_out=ssq[:, c])
                    e_s.inc(ins)
                    W.wait(sc, e_g, g_var)
                    ins = sc.activation(rstd[:, c], rstd[:, c], AF.Ln, bias=cvec[:, 0:1])
                    m = e_s.inc(ins)
                    W.wait(sc, e_s, m)
                    ins = sc.activation(rstd[:, c], rstd[:, c], AF.Exp, scale=-0.5)
                    e_s.inc(ins)
                    W.wait(sc, e_g, g_negmr)
                    ins = sc.activation(xn_[nbuf][:, :], xf[buf][:, :], AF.Identity,
                                        bias=negmr[:, c], scale=rstd[:, c])
                    tile_ln_done[ti] = e_s.inc(ins)

                @blk.gpsimd
                def _(g, ti=ti, tt=tt, s_sq=s_sq, s_exp=s_exp):
                    c = slice(tt, tt + 1)
                    W.wait(g, e_v)
                    W.wait(g, e_s, s_sq)
                    W.wait(g, e_g)
                    ins = g.tensor_tensor(out=ssum[:, c], in0=ssum[:, c],
                                          in1=cvec[:, 1:2], op=ALU.mult)
                    e_g.inc(ins)
                    W.wait(g, e_g)
                    ins = g.tensor_tensor(out=negmr[:, c], in0=ssq[:, c],
                                          in1=cvec[:, 1:2], op=ALU.mult)
                    e_g.inc(ins)
                    W.wait(g, e_g)
                    ins = g.tensor_tensor(out=rstd[:, c], in0=ssum[:, c],
                                          in1=ssum[:, c], op=ALU.mult)
                    e_g.inc(ins)
                    W.wait(g, e_g)
                    ins = g.tensor_tensor(out=rstd[:, c], in0=negmr[:, c],
                                          in1=rstd[:, c], op=ALU.subtract)
                    e_g.inc(ins)
                    W.wait(g, e_s, s_exp)
                    ins = g.tensor_tensor(out=negmr[:, c], in0=ssum[:, c],
                                          in1=rstd[:, c], op=ALU.mult)
                    e_g.inc(ins)
                    W.wait(g, e_g)
                    ins = g.tensor_tensor(out=negmr[:, c], in0=negmr[:, c],
                                          in1=cvec[:, 2:3], op=ALU.mult)
                    e_g.inc(ins)
                g_base[ti] = e_g.v

                dst, off, stride = xnT_of(ti)
                for g8 in range(8):
                    gid = ti * 8 + g8

                    @blk.tensor
                    def _(t, ti=ti, nbuf=nbuf, g8=g8, gid=gid):
                        W.wait(t, d_sy, p0_sy)
                        W.wait(t, e_s, tile_ln_done[ti])
                        if gid >= 2:
                            W.wait(t, e_v, grp_evac[gid - 2])
                        pt = ptr if g8 % 2 == 0 else ptr2
                        for j in range(4):
                            dc = g8 * 4 + j
                            ins = t.matmul(pt[:, j * 128:(j + 1) * 128],
                                           xn_[nbuf][:, dc * 128:(dc + 1) * 128],
                                           identb[:, :], start=True, stop=True)
                        tr_mm_done[ti] = e_t.inc(ins)

                    @blk.vector
                    def _(v, ti=ti, g8=g8, gid=gid, dst=dst, off=off, stride=stride):
                        W.wait(v, e_t, tr_mm_done[ti])
                        pt = ptr if g8 % 2 == 0 else ptr2
                        for j in range(4):
                            dc = g8 * 4 + j
                            ins = v.tensor_copy(
                                out=dst[:, dc * stride + off: dc * stride + off + 128],
                                in_=pt[:, j * 128:(j + 1) * 128])
                        grp_evac[gid] = e_v.inc(ins)
                if ti % 8 == 7:
                    pair_consumer(ti // 8)

        CKV = sb("CKV", [128, 4 * S], BF16, 26656)

        with ExitStack() as ph:
            E1 = ph.enter_context
            wdkv = sb("wdkv", [128, NDC * DC], BF16, 59648)
            wrk = sb("wrk", [128, NDC * DR], BF16, 92416)
            xf = [sb(f"xfa{i}", [128, D], F32, (96512, 112896, 211200)[i]) for i in range(3)]
            xn_ = [sb(f"xna{i}", [128, D], BF16, 129280 + i * 8192) for i in range(2)]
            xnT = [sb(f"xnTa{i}", [128, NDC * 512], BF16, 145664 + i * 32768)
                   for i in range(2)]
            dump = sb("dumpa", [128, D], F32, 211200)
            ssum = sb("ssuma", [128, 4], F32, 227584)
            ssq = sb("ssqa", [128, 4], F32, 227648)
            rstd = sb("rstda", [128, 4], F32, 227712)
            negmr = sb("negmra", [128, 4], F32, 227776)
            ptr = E1(nc.psum_tensor("ptra", [128, 512], F32))
            ptr2 = E1(nc.psum_tensor("ptr2a", [128, 512], F32))
            pck = [E1(nc.psum_tensor(f"pcka{i}", [128, 1024], F32)) for i in range(2)]
            pkr = E1(nc.psum_tensor("pkra", [64, 1024], F32))

            ck_evac = {"v": 0, "kr": 0}

            def consumers(pair):
                for ccp in range(2):
                    @blk.tensor
                    def _(t, ccp=ccp):
                        W.wait(t, d_gp)
                        W.wait(t, e_v)
                        W.wait(t, e_s, ck_evac["v"])
                        for cc in (2 * ccp, 2 * ccp + 1):
                            for dc in range(NDC):
                                for hf in range(2):
                                    ins = t.matmul(
                                        pck[cc % 2][:, hf * 512:hf * 512 + 512],
                                        wdkv[:, dc * DC + cc * 128: dc * DC + cc * 128 + 128],
                                        xnT[hf][:, dc * 512:(dc + 1) * 512],
                                        start=(dc == 0), stop=(dc == NDC - 1))
                            e_t.inc(ins)

                    @blk.scalar
                    def _(sc, ccp=ccp, pair=pair):
                        base = e_t.v - 2
                        for i, cc in enumerate((2 * ccp, 2 * ccp + 1)):
                            W.wait(sc, e_t, base + i + 1)
                            ins = sc.activation(
                                CKV[:, cc * S + pair * 1024: cc * S + pair * 1024 + 1024],
                                pck[cc % 2][:, :], AF.Identity)
                            ck_evac["v"] = e_s.inc(ins)

                @blk.tensor
                def _(t, pair=pair):
                    W.wait(t, e_s, ck_evac["kr"])
                    for dc in range(NDC):
                        for hf in range(2):
                            ins = t.matmul(pkr[:, hf * 512:hf * 512 + 512],
                                           wrk[:, dc * DR:(dc + 1) * DR],
                                           xnT[hf][:, dc * 512:(dc + 1) * 512],
                                           start=(dc == 0), stop=(dc == NDC - 1))
                    e_t.inc(ins)

                @blk.scalar
                def _(sc, pair=pair):
                    W.wait(sc, d_sy, p0_sy)
                    W.wait(sc, e_t)
                    ins = sc.activation(Kr2[0:64, pair * 1024:(pair + 1) * 1024],
                                        pkr[0:64, :], AF.Identity,
                                        bias=b_rk_t[0:64, 0:1])
                    ck_evac["kr"] = e_s.inc(ins)

            with nc.Block("p1a") as blk:
                @blk.gpsimd
                def _(g):
                    g.dma_start(out=wdkv[:, :], in_=w_dkv[:, :]).then_inc(d_gp.h, 16)
                    g.dma_start(out=wrk[:, :], in_=w_rk[:, :]).then_inc(d_gp.h, 16)
                    d_gp.v += 32

                def xnT_of(ti):
                    return xnT[(ti // 4) % 2], (ti % 4) * 128, 512

                import os
                if not os.environ.get("SKIP_P1A"):
                    ln_transpose(blk, xb, 32, xf, xn_, xnT_of, dump,
                                 ssum, ssq, rstd, negmr, ptr, ptr2, consumers)

                @blk.sync
                def _(sy):
                    W.wait(sy, e_s)
                    sy.dma_start(out=Kr2[64:128, :], in_=Kr2[0:64, :]).then_inc(d_sy.h, 16)
                    d_sy.v += 16
                    if t_ckv is not None:
                        sy.dma_start(out=t_ckv, in_=CKV[:, :]).then_inc(d_sy.h, 16)
                        d_sy.v += 16
                    if t_kr is not None:
                        W.wait(sy, d_sy)
                        sy.dma_start(out=t_kr, in_=Kr2[:, :]).then_inc(d_sy.h, 16)
                        d_sy.v += 16

        if upto == "p1a":
            return nc, tap_aps

        QT_ = sb("QT_", [128, H * QT], BF16, 59648)
        QrT = sb("QrT", [128, 16 * QT], BF16, 196576)

        with ExitStack() as ph:
            E1 = ph.enter_context
            xnq = sb("xnq", [128, NDC * QT], BF16, 59648)
            CQ = sb("CQ", [128, 12 * QT], BF16, 125184)

            with ExitStack() as ph2:
                E2 = ph2.enter_context
                xf = [sb(f"xfb{i}", [128, D], F32, 125184 + i * 16384) for i in range(2)]
                xn_ = [sb(f"xnb{i}", [128, D], BF16, 157952 + i * 8192) for i in range(2)]
                dump = sb("dumpb", [128, D], F32, 174592)
                ssum = sb("ssumb", [128, 4], F32, 174336)
                ssq = sb("ssqb", [128, 4], F32, 174400)
                rstd = sb("rstdb", [128, 4], F32, 174464)
                negmr = sb("negmrb", [128, 4], F32, 174528)
                ptr = E2(nc.psum_tensor("ptrb", [128, 512], F32))
                ptr2 = E2(nc.psum_tensor("ptr2b", [128, 512], F32))

                with nc.Block("p1b_ln") as blk:
                    ln_transpose(blk, xq, 8, xf, xn_,
                                 lambda ti: (xnq, ti * 128, QT), dump,
                                 ssum, ssq, rstd, negmr, ptr, ptr2, lambda p: None)

                    @blk.sync
                    def _(sy):
                        W.wait(sy, e_v)
                        sy.dma_start(out=sc_xnT.rearrange("(dc p) q -> p dc q", p=128),
                                     in_=xnq.rearrange("p (dc q) -> p dc q", dc=NDC)
                                     ).then_inc(d_st.h, 16)
                        d_st.v += 16
                        if t_lnst is not None:
                            W.wait(sy, e_s)
                            for i, src in enumerate((ssum, ssq, rstd, negmr)):
                                sy.dma_start(out=t_lnst[:, i * 4:(i + 1) * 4],
                                             in_=src[:, :]).then_inc(d_sy.h, 16)
                                d_sy.v += 16
                        if t_xnq is not None:
                            sy.dma_start(out=t_xnq, in_=xnq[:, :]).then_inc(d_sy.h, 16)
                            d_sy.v += 16

            if upto == "p1b_ln":
                return nc, tap_aps

            with ExitStack() as ph2:
                E2 = ph2.enter_context
                wdq = [sb(f"wdq{i}", [128, NDC * 512], BF16, 149760 + i * 32768)
                       for i in range(2)]
                pcq = [E2(nc.psum_tensor(f"pcqb{i}", [128, 1024], F32)) for i in range(2)]

                with nc.Block("p1b_cq") as blk:
                    wdq_dma = {}
                    cq_mm = {}
                    cq_ev = {}

                    def load_strip(st):
                        @blk.gpsimd
                        def _(g, st=st):
                            if st >= 2:
                                W.wait(g, e_t, cq_mm[st - 2])
                            g.dma_start(out=wdq[st % 2][:, :],
                                        in_=w_dq[st, :, :]).then_inc(gw[st % 2].h, 16)
                            gw[st % 2].v += 16
                            wdq_dma[st] = (st % 2, gw[st % 2].v)

                    load_strip(0)
                    load_strip(1)
                    for st in range(3):
                        for j in range(4):
                            c1c = st * 4 + j

                            @blk.tensor
                            def _(t, st=st, j=j, c1c=c1c):
                                W.wait(t, gw[wdq_dma[st][0]], wdq_dma[st][1])
                                W.wait(t, e_s, cq_ev.get(c1c - 2, 0))
                                pb = pcq[c1c % 2]
                                for dc in range(NDC):
                                    for hf in range(2):
                                        ins = t.matmul(
                                            pb[:, hf * 512:hf * 512 + 512],
                                            wdq[st % 2][:, dc * 512 + j * 128:
                                                        dc * 512 + j * 128 + 128],
                                            xnq[:, dc * QT + hf * 512:
                                                dc * QT + hf * 512 + 512],
                                            start=(dc == 0), stop=(dc == NDC - 1))
                                cq_mm[st] = e_t.inc(ins)

                            @blk.scalar
                            def _(sc, c1c=c1c):
                                W.wait(sc, d_sy, p0_sy)
                                W.wait(sc, e_t, cq_mm[st])
                                ins = sc.activation(CQ[:, c1c * QT:(c1c + 1) * QT],
                                                    pcq[c1c % 2][:, :], AF.Identity,
                                                    bias=b_dq_t[:, c1c:c1c + 1])
                                cq_ev[c1c] = e_s.inc(ins)
                        if st + 2 < 3:
                            load_strip(st + 2)

                    if t_cq is not None:
                        @blk.sync
                        def _(sy):
                            W.wait(sy, e_s)
                            sy.dma_start(out=t_cq, in_=CQ[:, :]).then_inc(d_sy.h, 16)
                            d_sy.v += 16

            if upto == "p1b_cq":
                return nc, tap_aps

            with ExitStack() as ph2:
                E2 = ph2.enter_context
                wuq = [sb(f"wuq{i}", [128, 12 * 512], BF16, 149760 + i * 12288)
                       for i in range(2)]
                pq = [E2(nc.psum_tensor(f"pqb{i}", [128, 1024], F32)) for i in range(2)]

                with nc.Block("p1b_q") as blk:
                    q_dma = {}
                    q_mm = {}
                    q_ev = {}

                    def load_qstrip(si):
                        @blk.gpsimd
                        def _(g, si=si):
                            if si >= 2:
                                W.wait(g, e_t, q_mm[si - 2])
                            src = w_uq[si, :, :] if si < 8 else w_rq[si - 8, :, :]
                            g.dma_start(out=wuq[si % 2][:, :],
                                        in_=src).then_inc(gw[si % 2].h, 16)
                            gw[si % 2].v += 16
                            q_dma[si] = (si % 2, gw[si % 2].v)

                    load_qstrip(0)
                    load_qstrip(1)
                    nchunk = 0
                    for si in range(12):
                        for j in range(4):
                            @blk.tensor
                            def _(t, si=si, j=j, nchunk=nchunk):
                                W.wait(t, gw[q_dma[si][0]], q_dma[si][1])
                                W.wait(t, e_s, q_ev.get(nchunk - 2, 0))
                                pb = pq[nchunk % 2]
                                for cc in range(12):
                                    for hf in range(2):
                                        ins = t.matmul(
                                            pb[:, hf * 512:hf * 512 + 512],
                                            wuq[si % 2][:, cc * 512 + j * 128:
                                                        cc * 512 + j * 128 + 128],
                                            CQ[:, cc * QT + hf * 512:
                                               cc * QT + hf * 512 + 512],
                                            start=(cc == 0), stop=(cc == 11))
                                q_mm[si] = e_t.inc(ins)

                            @blk.scalar
                            def _(sc, si=si, j=j, nchunk=nchunk):
                                W.wait(sc, d_st)
                                W.wait(sc, e_t, q_mm[si])
                                if si < 8:
                                    hdc = si * 4 + j
                                    ins = sc.activation(QT_[:, hdc * QT:(hdc + 1) * QT],
                                                        pq[nchunk % 2][:, :], AF.Identity,
                                                        bias=b_uq_t[:, hdc:hdc + 1])
                                else:
                                    rc = (si - 8) * 4 + j
                                    ins = sc.activation(QrT[:, rc * QT:(rc + 1) * QT],
                                                        pq[nchunk % 2][:, :], AF.Identity,
                                                        bias=b_rq_t[:, rc:rc + 1])
                                q_ev[nchunk] = e_s.inc(ins)
                            nchunk += 1
                        if si + 2 < 12:
                            load_qstrip(si + 2)

                    @blk.vector
                    def _(v):
                        W.wait(v, e_s)
                        for h in range(H):
                            ins = v.reduce_max(aQ[:, h:h + 1], QT_[:, h * QT:(h + 1) * QT],
                                               axis=AX.X, apply_absolute_value=True)
                        for rc in range(16):
                            ins = v.reduce_max(aQr[:, rc:rc + 1],
                                               QrT[:, rc * QT:(rc + 1) * QT],
                                               axis=AX.X, apply_absolute_value=True)
                        e_v.inc(ins)

                    if t_qt is not None:
                        @blk.sync
                        def _(sy):
                            W.wait(sy, e_s)
                            sy.dma_start(out=t_qt, in_=QT_[:, :]).then_inc(d_sy.h, 16)
                            sy.dma_start(out=t_qrt, in_=QrT[:, :]).then_inc(d_sy.h, 16)
                            d_sy.v += 32

        if upto == "p1b_q":
            return nc, tap_aps

        with ExitStack() as ph:
            E1 = ph.enter_context
            wkv = sb("wkv", [128, 4 * D], BF16, 125184)
            kev = [sb(f"kev{i}", [128, 2048], BF16, 157952 + i * 4096) for i in range(2)]

            with nc.Block("p2k") as blk:
                @blk.gpsimd
                def _(g):
                    g.dma_start(out=wkv[:, :], in_=w_uk[:, :]).then_inc(d_gp.h, 16)
                    d_gp.v += 16

                with ExitStack() as ph2:
                    pk = [ph2.enter_context(nc.psum_tensor(f"pk{i}", [128, 2048], F32))
                          for i in range(2)]
                    k_ev = {}
                    k_red = {}
                    k_st = {}
                    for hdc in range(H):
                        @blk.tensor
                        def _(t, hdc=hdc):
                            W.wait(t, d_gp)
                            for half in range(2):
                                W.wait(t, e_v, k_ev.get((hdc - 1, half), 0))
                                for cc in range(4):
                                    for tq in range(4):
                                        ins = t.matmul(
                                            pk[half][:, tq * 512:(tq + 1) * 512],
                                            wkv[:, cc * D + hdc * 128:
                                                cc * D + hdc * 128 + 128],
                                            CKV[:, cc * S + half * 2048 + tq * 512:
                                                cc * S + half * 2048 + tq * 512 + 512],
                                            start=(cc == 0), stop=(cc == 3))
                                e_t.inc(ins)

                        @blk.vector
                        def _(v, hdc=hdc):
                            W.wait(v, d_sy, p0_sy)
                            base = e_t.v - 2
                            for half in range(2):
                                W.wait(v, e_t, base + half + 1)
                                W.wait(v, d_st, k_st.get(hdc - 1, 0))
                                W.wait(v, e_v)
                                ins = v.tensor_scalar(out=kev[half][:, :],
                                                      in0=pk[half][:, :],
                                                      scalar1=b_uk_t[:, hdc:hdc + 1],
                                                      scalar2=None, op0=ALU.add)
                                k_ev[(hdc, half)] = e_v.inc(ins)
                                W.wait(v, e_v)
                                ins = v.reduce_max(aK[:, hdc:hdc + 1] if half == 0
                                                   else cvec[:, 3:4],
                                                   kev[half][:, :], axis=AX.X,
                                                   apply_absolute_value=True)
                                k_red[(hdc, half)] = e_v.inc(ins)
                            W.wait(v, e_v)
                            ins = v.tensor_tensor(out=aK[:, hdc:hdc + 1],
                                                  in0=aK[:, hdc:hdc + 1],
                                                  in1=cvec[:, 3:4], op=ALU.max)
                            e_v.inc(ins)

                        @blk.sync
                        def _(sy, hdc=hdc):
                            W.wait(sy, e_v, k_ev[(hdc, 1)])
                            sy.dma_start(out=sc_KT[hdc * 128:(hdc + 1) * 128, 0:2048],
                                         in_=kev[0][:, :]).then_inc(d_st.h, 16)
                            sy.dma_start(out=sc_KT[hdc * 128:(hdc + 1) * 128, 2048:4096],
                                         in_=kev[1][:, :]).then_inc(d_st.h, 16)
                            d_st.v += 32
                            k_st[hdc] = d_st.v

                    @blk.vector
                    def _(v):
                        ins = v.reduce_max(aKr[0:64, 0:1], Kr2[0:64, :], axis=AX.X,
                                           apply_absolute_value=True)
                        e_v.inc(ins)

                    @blk.sync
                    def _(sy):
                        W.wait(sy, e_v)
                        sy.dma_start(out=aKr[64:128, :], in_=aKr[0:64, :]
                                     ).then_inc(d_sy.h, 16)
                        d_sy.v += 16

            with nc.Block("p2v") as blk:
                @blk.gpsimd
                def _(g):
                    g.dma_start(out=wkv[:, :], in_=w_uv[:, :]).then_inc(d_gp.h, 16)
                    d_gp.v += 16

                with ExitStack() as ph2:
                    pvb = ph2.enter_context(nc.psum_tensor("pvb", [128, NDC], F32))

                    @blk.tensor
                    def _(t):
                        W.wait(t, d_gp)
                        for hdc in range(NDC):
                            for cc in range(4):
                                ins = t.matmul(pvb[:, hdc:hdc + 1],
                                               wkv[:, cc * D + hdc * 128:
                                                   cc * D + hdc * 128 + 128],
                                               b_dkv_t[:, cc:cc + 1],
                                               start=(cc == 0), stop=(cc == 3))
                        e_t.inc(ins)

                    @blk.vector
                    def _(v):
                        W.wait(v, e_t)
                        ins = v.tensor_tensor(out=vbias[:, :], in0=vbias[:, :],
                                              in1=pvb[:, :], op=ALU.add)
                        e_v.inc(ins)

                with ExitStack() as ph2:
                    pv = [ph2.enter_context(nc.psum_tensor(f"pv{i}", [128, 2048], F32))
                          for i in range(2)]
                    v_ev = {}
                    v_st = {}
                    for ts in range(32):
                        @blk.tensor
                        def _(t, ts=ts):
                            if ts == 0:
                                W.wait(t, e_v)
                            for half in range(2):
                                W.wait(t, e_v, v_ev.get((ts - 1, half), 0))
                                for cc in range(4):
                                    for dq in range(4):
                                        ins = t.matmul(
                                            pv[half][:, dq * 512:(dq + 1) * 512],
                                            CKV[:, cc * S + ts * 128:
                                                cc * S + ts * 128 + 128],
                                            wkv[:, cc * D + half * 2048 + dq * 512:
                                                cc * D + half * 2048 + dq * 512 + 512],
                                            start=(cc == 0), stop=(cc == 3))
                                e_t.inc(ins)

                        @blk.scalar
                        def _(sc, ts=ts):
                            base = e_t.v - 2
                            for half in range(2):
                                W.wait(sc, e_t, base + half + 1)
                                if ts == 0:
                                    W.wait(sc, d_st)
                                W.wait(sc, d_st, v_st.get(ts - 1, 0))
                                ins = sc.activation(kev[half][:, :], pv[half][:, :],
                                                    AF.Identity)
                                v_ev[(ts, half)] = e_s.inc(ins)

                        @blk.sync
                        def _(sy, ts=ts):
                            W.wait(sy, e_v, v_ev[(ts, 1)])
                            sy.dma_start(out=sc_V[ts * 128:(ts + 1) * 128, 0:2048],
                                         in_=kev[0][:, :]).then_inc(d_st.h, 16)
                            sy.dma_start(out=sc_V[ts * 128:(ts + 1) * 128, 2048:4096],
                                         in_=kev[1][:, :]).then_inc(d_st.h, 16)
                            d_st.v += 32
                            v_st[ts] = d_st.v

                if t_ab is not None:
                    @blk.scalar
                    def _(sc):
                        W.wait(sc, e_v)
                        sc.dma_start(out=t_ab[:, 0:NDC], in_=aK[:, :]).then_inc(d_sc.h, 16)
                        sc.dma_start(out=t_ab[:, NDC:2 * NDC], in_=vbias[:, :]
                                     ).then_inc(d_sc.h, 16)
                        d_sc.v += 32

        if upto == "p2":
            return nc, tap_aps

        attnT = sb("attnT", [128, H * QT], BF16, 125184)

        with ExitStack() as ph:
            E1 = ph.enter_context
            kt = [sb(f"kt{i}", [128, 2048], BF16, 26656 + i * 4096) for i in range(2)]
            vh = [sb(f"vh{i}", [128, 2048], BF16, 34848 + i * 4096) for i in range(2)]
            PT = [sb(f"PT{i}", [128, QT], BF16,
                     (190720, 192768, 47136)[i]) for i in range(3)]
            rz = sb("rz", [128, QT], F32, 43040)
            Zacc = sb("Zacc", [128, QT], F32, 49184)
            qr_cur = [sb(f"qrc{i}", [128, QT], BF16, 53280 + i * 2048) for i in range(2)]

            with ExitStack() as ph2:
                E2 = ph2.enter_context
                pU = E2(nc.psum_tensor("pU", [1, H], F32))
                pU2 = E2(nc.psum_tensor("pU2", [128, H], F32))
                with nc.Block("p3u") as blk:
                    @blk.tensor
                    def _(t):
                        W.wait(t, d_sy)
                        for h in range(H):
                            t.matmul(pU[0:1, h:h + 1], aQ[:, h:h + 1], aK[:, h:h + 1],
                                     start=True, stop=False)
                            ins = t.matmul(pU[0:1, h:h + 1],
                                           aQr[(h % 2) * 64:(h % 2) * 64 + 64,
                                               h // 2:h // 2 + 1],
                                           aKr[(h % 2) * 64:(h % 2) * 64 + 64, 0:1],
                                           start=False, stop=True)
                        e_t.inc(ins)

                    @blk.vector
                    def _(v):
                        W.wait(v, e_t)
                        ins = v.tensor_scalar(out=nurow[0:1, :], in0=pU[0:1, :],
                                              scalar1=-SCALER, scalar2=None, op0=ALU.mult)
                        e_v.inc(ins)

                    @blk.tensor
                    def _(t):
                        W.wait(t, e_v)
                        ins = t.matmul(pU2[:, :], ones128[0:1, :], nurow[0:1, :],
                                       start=True, stop=True)
                        e_t.inc(ins)

                    @blk.vector
                    def _(v):
                        W.wait(v, e_t)
                        ins = v.tensor_copy(out=negU[:, :], in_=pU2[:, :])
                        e_v.inc(ins)

                    if t_negu is not None:
                        @blk.sync
                        def _(sy):
                            W.wait(sy, e_v)
                            sy.dma_start(out=t_negu, in_=negU[:, :]).then_inc(d_sy.h, 16)
                            d_sy.v += 16

            with ExitStack() as ph2:
                E2 = ph2.enter_context
                pS3 = [E2(nc.psum_tensor(f"pS{i}", [128, QT], F32)) for i in range(3)]
                pPV = E2(nc.psum_tensor("pPV", [128, QT], F32))

                with nc.Block("p3") as blk:
                    kv_dma = {}
                    qr_dma = {}
                    pe_head_done = {}
                    rot = 0
                    prev_exp29 = prev_exp31 = 0
                    attn_evac = {}
                    rz_done = {}
                    prev_head = None

                    for h in range(H):
                        qb = h % 2

                        @blk.sync
                        def _(sy, h=h, qb=qb):
                            if h == 0:
                                W.wait(sy, d_st)
                            if h > 0:
                                W.wait(sy, e_t, prev_head["zpv"](15))
                            sy.dma_start(out=kt[0][:, :],
                                         in_=sc_KT[h * 128:(h + 1) * 128, 0:2048]
                                         ).then_inc(kv[0].h, 16)
                            sy.dma_start(out=vh[0].rearrange("p (ks d) -> p ks d", ks=16),
                                         in_=sc_V.rearrange("(ks p) o -> p ks o", p=128)
                                         [:, 0:16, h * 128:(h + 1) * 128]
                                         ).then_inc(kv[0].h, 16)
                            kv[0].v += 32
                            kv_dma[(h, 0)] = kv[0].v
                            src = QrT[(h % 2) * 64:(h % 2) * 64 + 64,
                                      (h // 2) * QT:(h // 2) * QT + QT]
                            sy.dma_start(out=qr_cur[qb][0:64, :], in_=src
                                         ).then_inc(qr.h, 16)
                            sy.dma_start(out=qr_cur[qb][64:128, :], in_=src
                                         ).then_inc(qr.h, 16)
                            qr.v += 32
                            qr_dma[h] = qr.v
                            if h > 0:
                                W.wait(sy, e_t, prev_head["zpv"](31))
                            sy.dma_start(out=kt[1][:, :],
                                         in_=sc_KT[h * 128:(h + 1) * 128, 2048:4096]
                                         ).then_inc(kv[1].h, 16)
                            sy.dma_start(out=vh[1].rearrange("p (ks d) -> p ks d", ks=16),
                                         in_=sc_V.rearrange("(ks p) o -> p ks o", p=128)
                                         [:, 16:32, h * 128:(h + 1) * 128]
                                         ).then_inc(kv[1].h, 16)
                            kv[1].v += 32
                            kv_dma[(h, 1)] = kv[1].v

                        t0 = e_t.v
                        s0 = e_s.v
                        g0 = e_g.v
                        v0 = e_v.v
                        r = rot
                        B = lambda ks, r=r: (ks + r) % 3
                        bb = B(30)
                        exp_done = lambda ks, s0=s0: s0 + ks + 1
                        vadd_done = (lambda ks, g0=g0, v0=v0:
                                     (v0 if ks % 2 == 0 else g0) + ks // 2 + 1)
                        pair_done = lambda p, t0=t0: t0 + (p + 1 if p < 2 else 3 * p - 1)
                        zpv_done = (lambda ks, t0=t0:
                                    t0 + (3 + 3 * (ks // 2) + ks % 2 if ks <= 27
                                          else 17 + ks))

                        @blk.tensor
                        def _(t, h=h, qb=qb, t0=t0, s0=s0):
                            def kpass(ks):
                                kh, kc = ks // 16, (ks % 16) * 128
                                pS = pS3[B(ks)]
                                W.wait(t, kv[kh], kv_dma[(h, kh)])
                                if ks < 3:
                                    if h == 0:
                                        W.wait(t, d_sy)
                                    elif ks == 0:
                                        W.wait(t, e_s, prev_exp29)
                                    elif ks == 2:
                                        W.wait(t, e_s, prev_exp31)
                                    else:
                                        W.wait(t, e_s, rz_done[h - 1])
                                else:
                                    W.wait(t, e_s, exp_done(ks - 3))
                                for hf in range(2):
                                    t.matmul(pS[:, hf * 512:hf * 512 + 512],
                                             kt[kh][:, kc:kc + 128],
                                             QT_[:, h * QT + hf * 512:
                                                 h * QT + hf * 512 + 512],
                                             start=True, stop=False)

                            def spair(p):
                                ks, ks1 = 2 * p, 2 * p + 1
                                kpass(ks)
                                kpass(ks1)
                                W.wait(t, qr, qr_dma[h])
                                for hf in range(2):
                                    t.matmul(pS3[B(ks)][:, hf * 512:hf * 512 + 512],
                                             Kr2[0:64, ks * 128:(ks + 1) * 128],
                                             qr_cur[qb][0:64, hf * 512:hf * 512 + 512],
                                             start=False, stop=True)
                                    ins = t.matmul(
                                        pS3[B(ks1)][:, hf * 512:hf * 512 + 512],
                                        Kr2[64:128, ks1 * 128:(ks1 + 1) * 128],
                                        qr_cur[qb][64:128, hf * 512:hf * 512 + 512],
                                        start=False, stop=True)
                                e_t.inc(ins)

                            def pv(ks):
                                kh, kc = ks // 16, (ks % 16) * 128
                                W.wait(t, e_s, exp_done(ks))
                                if ks == 0:
                                    W.wait(t, e_v, attn_evac.get(h - 1, 0))
                                for hf in range(2):
                                    ins = t.matmul(pPV[:, hf * 512:hf * 512 + 512],
                                                   vh[kh][:, kc:kc + 128],
                                                   PT[B(ks)][:, hf * 512:hf * 512 + 512],
                                                   start=(ks == 0), stop=(ks == 31))
                                e_t.inc(ins)

                            spair(0)
                            spair(1)
                            for p in range(14):
                                pv(2 * p)
                                pv(2 * p + 1)
                                spair(p + 2)
                            pv(28)
                            pv(29)
                            pv(30)
                            pv(31)
                            W.wait(t, e_v, v0 + 16)
                            W.wait(t, e_s, exp_done(31))
                            for hf in range(2):
                                for j in (29, 30, 31):
                                    ins = t.matmul(pS3[bb][:, hf * 512:hf * 512 + 512],
                                                   ones128[:, :],
                                                   PT[B(j)][:, hf * 512:hf * 512 + 512],
                                                   start=(j == 29), stop=(j == 31))
                            e_t.inc(ins)

                        @blk.scalar
                        def _(sc, h=h, t0=t0):
                            for ks in range(32):
                                W.wait(sc, e_t, pair_done(ks // 2))
                                if ks >= 3:
                                    W.wait(sc, e_t, zpv_done(ks - 3))
                                    W.wait(sc, e_v if (ks - 3) % 2 == 0 else e_g,
                                           vadd_done(ks - 3))
                                elif h > 0:
                                    W.wait(sc, e_v, attn_evac[h - 1])
                                sc.activation(PT[B(ks)][:, 0:512],
                                              pS3[B(ks)][:, 0:512], AF.Exp,
                                              scale=SCALER, bias=negU[:, h:h + 1])
                                ins = sc.activation(PT[B(ks)][:, 512:1024],
                                                    pS3[B(ks)][:, 512:1024], AF.Exp,
                                                    scale=SCALER, bias=negU[:, h:h + 1])
                                e_s.inc(ins)
                            W.wait(sc, e_t, t0 + 49)
                            ins = sc.activation(rz[:, :], pS3[bb][:, :], AF.Ln)
                            m = e_s.inc(ins)
                            W.wait(sc, e_s, m)
                            ins = sc.activation(rz[:, :], rz[:, :], AF.Exp, scale=-1.0)
                            rz_done[h] = e_s.inc(ins)

                        @blk.gpsimd
                        def _(g, h=h, s0=s0):
                            for ks in range(1, 30, 2):
                                W.wait(g, e_s, exp_done(ks))
                                W.wait(g, e_g)
                                if ks == 1:
                                    ins = g.tensor_copy(out=rz[:, :], in_=PT[B(1)][:, :])
                                else:
                                    ins = g.tensor_tensor(out=rz[:, :], in0=rz[:, :],
                                                          in1=PT[B(ks)][:, :], op=ALU.add)
                                e_g.inc(ins)

                        @blk.vector
                        def _(v, h=h, s0=s0, g0=g0, t0=t0):
                            for ks in range(0, 30, 2):
                                W.wait(v, e_s, exp_done(ks))
                                W.wait(v, e_v)
                                if ks == 0:
                                    ins = v.tensor_copy(out=Zacc[:, :], in_=PT[B(0)][:, :])
                                else:
                                    ins = v.tensor_tensor(out=Zacc[:, :], in0=Zacc[:, :],
                                                          in1=PT[B(ks)][:, :], op=ALU.add)
                                e_v.inc(ins)
                            W.wait(v, e_g, g0 + 15)
                            W.wait(v, e_v)
                            W.wait(v, e_t, zpv_done(29))
                            ins = v.tensor_tensor(out=PT[B(29)][:, :], in0=Zacc[:, :],
                                                  in1=rz[:, :], op=ALU.add)
                            e_v.inc(ins)

                        assert e_t.v == t0 + 49 and e_s.v == s0 + 34
                        assert e_g.v == g0 + 15 and e_v.v == v0 + 16
                        prev_head = {"exp": exp_done, "zpv": zpv_done}
                        pe_head_done[h] = zpv_done(31)
                        prev_exp29 = exp_done(29)
                        prev_exp31 = exp_done(31)
                        rot = (rot + 2) % 3

                        @blk.vector
                        def _(v, h=h):
                            W.wait(v, e_s, rz_done[h])
                            ins = v.tensor_tensor(out=attnT[:, h * QT:(h + 1) * QT],
                                                  in0=pPV[:, :], in1=rz[:, :],
                                                  op=ALU.mult)
                            m = e_v.inc(ins)
                            W.wait(v, e_v, m)
                            ins = v.tensor_scalar(out=attnT[:, h * QT:(h + 1) * QT],
                                                  in0=attnT[:, h * QT:(h + 1) * QT],
                                                  scalar1=vbias[:, h:h + 1],
                                                  scalar2=None, op0=ALU.add)
                            attn_evac[h] = e_v.inc(ins)

                    if t_attn is not None:
                        @blk.sync
                        def _(sy):
                            W.wait(sy, e_v)
                            sy.dma_start(out=t_attn, in_=attnT[:, :]).then_inc(d_sy.h, 16)
                            d_sy.v += 16

        if upto == "p3":
            return nc, tap_aps

        with ExitStack() as ph:
            E1 = ph.enter_context
            xnT2 = sb("xnT2", [128, NDC * QT], BF16, 59648)
            at = sb("at", [128, NDC * 256], BF16, 26656)
            bft = sb("bft", [128, 2 * D], BF16, 43040)
            xaT = sb("xaT", [128, 2 * QT], BF16, 207104)
            wo = [sb(f"wo{i}", [128, NDC * 128], BF16, 26656 + i * 8192) for i in range(2)]
            wf = [sb(f"wf{i}", [128, NDC * 128], BF16, 190720 + i * 8192) for i in range(2)]
            xr = [sb(f"xr{i}", [128, QT], F32, 18464 + i * 4096) for i in range(2)]
            ob = [sb("ob", [128, QT], F32, 211200)] * 2

            with nc.Block("p4") as blk:
                @blk.sync
                def _(sy):
                    W.wait(sy, d_st)
                    sy.dma_start(out=xnT2.rearrange("p (dc q) -> p dc q", dc=NDC),
                                 in_=sc_xnT.rearrange("(dc p) q -> p dc q", p=128)
                                 ).then_inc(d_sy.h, 16)
                    d_sy.v += 16

                @blk.gpsimd
                def _(g):
                    g.dma_start(out=at[:, :], in_=A_T[:, :]).then_inc(d_gp.h, 16)
                    g.dma_start(out=bft[:, :], in_=Bf_T[:, :]).then_inc(d_gp.h, 16)
                    d_gp.v += 32

                with ExitStack() as ph2:
                    pxa = ph2.enter_context(nc.psum_tensor("pxa", [128, QT], F32))
                    xa_ev = {}
                    for erc in range(2):
                        @blk.tensor
                        def _(t, erc=erc):
                            W.wait(t, d_gp)
                            W.wait(t, d_sy)
                            if erc == 1:
                                W.wait(t, e_v, xa_ev[0])
                            for dc in range(NDC):
                                for hf in range(2):
                                    ins = t.matmul(
                                        pxa[:, hf * 512:hf * 512 + 512],
                                        at[:, dc * 256 + erc * 128:
                                           dc * 256 + erc * 128 + 128],
                                        xnT2[:, dc * QT + hf * 512:
                                             dc * QT + hf * 512 + 512],
                                        start=(dc == 0), stop=(dc == NDC - 1))
                            e_t.inc(ins)

                        @blk.vector
                        def _(v, erc=erc):
                            W.wait(v, e_t)
                            ins = v.tensor_scalar(out=xaT[:, erc * QT:(erc + 1) * QT],
                                                  in0=pxa[:, :],
                                                  scalar1=wscale[:, erc:erc + 1],
                                                  scalar2=None, op0=ALU.mult)
                            xa_ev[erc] = e_v.inc(ins)
                    if t_xa is not None:
                        @blk.sync
                        def _(sy):
                            W.wait(sy, e_v)
                            sy.dma_start(out=t_xa, in_=xaT[:, :]).then_inc(d_sy.h, 16)
                            d_sy.v += 16

                with ExitStack() as ph2:
                    po = [ph2.enter_context(nc.psum_tensor(f"po{i}", [128, QT], F32))
                          for i in range(2)]
                    wdma = {}
                    xdma = {}
                    mm_done = {}
                    ev_done = {}
                    st_done = {}
                    for oc in range(NDC):
                        @blk.gpsimd
                        def _(g, oc=oc):
                            if oc >= 2:
                                W.wait(g, e_t, mm_done[oc - 2])
                            else:
                                W.wait(g, e_t)
                            g.dma_start(out=wo[oc % 2][:, :],
                                        in_=w_out[oc, :, :]).then_inc(gw[oc % 2].h, 16)
                            g.dma_start(out=wf[oc % 2][:, :],
                                        in_=WT[oc, :, :]).then_inc(gw[oc % 2].h, 16)
                            gw[oc % 2].v += 32
                            wdma[oc] = gw[oc % 2].v

                        @blk.sync
                        def _(sy, oc=oc):
                            if oc >= 2:
                                W.wait(sy, e_v, ev_done[oc - 2])
                            sy.dma_start(out=xr[oc % 2][:, :],
                                         in_=xTq[oc * 128:(oc + 1) * 128, :]
                                         ).then_inc(xs[oc % 2].h, 16)
                            xs[oc % 2].v += 16
                            xdma[oc] = xs[oc % 2].v

                        @blk.tensor
                        def _(t, oc=oc):
                            W.wait(t, gw[oc % 2], wdma[oc])
                            if oc >= 2:
                                W.wait(t, e_v, ev_done[oc - 2])
                            else:
                                W.wait(t, e_v, xa_ev[1])
                            pb = po[oc % 2]
                            for dc in range(NDC):
                                for hf in range(2):
                                    t.matmul(pb[:, hf * 512:hf * 512 + 512],
                                             wo[oc % 2][:, dc * 128:(dc + 1) * 128],
                                             attnT[:, dc * QT + hf * 512:
                                                   dc * QT + hf * 512 + 512],
                                             start=(dc == 0), stop=False)
                            for dc in range(NDC):
                                for hf in range(2):
                                    t.matmul(pb[:, hf * 512:hf * 512 + 512],
                                             wf[oc % 2][:, dc * 128:(dc + 1) * 128],
                                             xnT2[:, dc * QT + hf * 512:
                                                  dc * QT + hf * 512 + 512],
                                             start=False, stop=False)
                            for erc in range(2):
                                for hf in range(2):
                                    ins = t.matmul(
                                        pb[:, hf * 512:hf * 512 + 512],
                                        bft[:, erc * D + oc * 128:
                                            erc * D + oc * 128 + 128],
                                        xaT[:, erc * QT + hf * 512:
                                            erc * QT + hf * 512 + 512],
                                        start=False, stop=(erc == 1))
                            mm_done[oc] = e_t.inc(ins)

                        @blk.vector
                        def _(v, oc=oc):
                            W.wait(v, e_t, mm_done[oc])
                            W.wait(v, xs[oc % 2], xdma[oc])
                            if oc >= 1:
                                W.wait(v, d_sc, st_done[oc - 1])
                            ins = v.scalar_tensor_tensor(out=ob[oc % 2][:, :],
                                                         in0=po[oc % 2][:, :],
                                                         scalar=bsum[:, oc:oc + 1],
                                                         in1=xr[oc % 2][:, :],
                                                         op0=ALU.add, op1=ALU.add)
                            ev_done[oc] = e_v.inc(ins)

                        @blk.scalar
                        def _(sc, oc=oc):
                            W.wait(sc, e_v, ev_done[oc])
                            sc.dma_start(out=outT[oc * 128:(oc + 1) * 128, :],
                                         in_=ob[oc % 2][:, :]).then_inc(d_sc.h, 16)
                            d_sc.v += 16
                            st_done[oc] = d_sc.v

                    @blk.scalar
                    def _(sc):
                        W.wait(sc, d_sc)

    return nc, tap_aps


_NC_CACHE = {}


def get_program(taps=(), upto="p4"):
    key = (tuple(sorted(taps)), upto)
    if key not in _NC_CACHE:
        _NC_CACHE[key] = build_program(taps, upto)
    return _NC_CACHE[key]


def make_in_maps(inputs):
    x = np.asarray(inputs["L0_attn_sequence"], dtype=np.float32)
    c = np.ascontiguousarray
    f32 = lambda k: np.asarray(inputs[k], np.float32)

    def colsT(b, nc_):
        return c(f32(b).reshape(nc_, 128).T)

    def lhsT_tiles(w, strips, swidth):
        K, M = w.shape
        return c(w.reshape(K // 128, 128, strips, swidth)
                 .transpose(2, 1, 0, 3).reshape(strips, 128, (K // 128) * swidth))

    shared = dict(
        w_dq=lhsT_tiles(f32("L0_attn_w_dq"), 3, 512),
        b_dq=colsT("L0_attn_b_dq", 12),
        w_uq=lhsT_tiles(f32("L0_attn_w_uq"), 8, 512),
        b_uq=colsT("L0_attn_b_uq", NDC),
        w_dkv=lhsT_tiles(f32("L0_attn_w_dkv"), 1, DC)[0],
        b_dkv=colsT("L0_attn_b_dkv", 4),
        w_uk=lhsT_tiles(f32("L0_attn_w_uk"), 1, D)[0],
        b_uk=colsT("L0_attn_b_uk", NDC),
        w_uv=lhsT_tiles(f32("L0_attn_w_uv"), 1, D)[0],
        b_uv=colsT("L0_attn_b_uv", NDC),
        w_rq=lhsT_tiles(f32("L0_attn_w_rq"), 4, 512),
        b_rq=colsT("L0_attn_b_rq", 16),
        w_rk=lhsT_tiles(f32("L0_attn_w_rk"), 1, DR)[0],
        b_rk=c(f32("L0_attn_b_rk").reshape(64, 1)),
        w_out=lhsT_tiles(f32("L0_attn_w_out"), NDC, 128),
        b_out=colsT("L0_attn_b_out", NDC),
        WT=lhsT_tiles(c(f32("L0_ffn_W").T), NDC, 128),
        fbias=colsT("L0_ffn_bias", NDC),
        A_T=lhsT_tiles(c(f32("L0_ffn_A").reshape(256, D).T), 1, 256)[0],
        Bf_T=lhsT_tiles(f32("L0_ffn_Bf").transpose(0, 2, 1).reshape(256, D), 1, D)[0],
        cs_rep=c(np.repeat(np.asarray(inputs["L0_ffn_coeffs"], np.float32), 16)
                 .reshape(2, 128).T),
        sc_rep=c(np.repeat(np.asarray(inputs["L0_ffn_scales"], np.float32), 16)
                 .reshape(2, 128).T),
        ident=np.eye(128, dtype=ml_dtypes.bfloat16),
    )
    in_maps = []
    for core in range(8):
        b, qs = core // 4, (core % 4) * QT
        m = dict(shared)
        m["xb"] = c(x[b])
        m["xq"] = c(x[b, qs:qs + QT, :])
        m["xTq"] = c(x[b, qs:qs + QT, :].T)
        in_maps.append(m)
    return in_maps


def kernel(**inputs):
    nc, _ = get_program()
    in_maps = make_in_maps(inputs)
    res = run_bass_kernel_spmd(nc, in_maps, core_ids=list(range(8)))
    out = np.empty((B, S, D), dtype=np.float32)
    for core in range(8):
        b, qs = core // 4, (core % 4) * QT
        out[b, qs:qs + QT, :] = res.results[core]["outT"].T
    return out
